# version 01 (final)
# baseline (speedup 1.0000x reference)
import os
import numpy as np
import ml_dtypes
import concourse.bass as bass
import concourse.mybir as mybir
from concourse.bass_utils import run_bass_kernel_spmd
from contextlib import ExitStack

F32 = mybir.dt.float32
BF16 = mybir.dt.bfloat16
AF = mybir.ActivationFunctionType
ALU = mybir.AluOpType
AX = mybir.AxisListType

B, S, D = 4, 4096, 1024
H, HD = 16, 64
NPAIR = 8
SH = S // 2
I_FFN = 2816
HH = 409
KPAD = 512
NCMP = 512
MAGIC = float(2.0 ** 23)
SCALE = 0.125
EPS_LN = 1e-5

ENGINES = ("sync", "gpsimd", "tensor", "scalar", "vector")


_DRAIN_OPS = {
    "activation", "tensor_scalar", "tensor_tensor", "tensor_copy", "tensor_add",
    "tensor_sub", "tensor_mul", "tensor_max", "tensor_relu", "tensor_reduce",
    "reduce_sum", "reduce_max", "reciprocal", "memset", "memzero",
    "scalar_tensor_tensor", "tensor_tensor_scan", "select", "copy_predicated",
    "iota", "sign", "square", "sqrt", "mul", "copy", "add", "tensor_single_scalar",
    "tensor_scalar_add", "tensor_scalar_sub", "tensor_scalar_mul",
    "tensor_scalar_max", "tensor_scalar_min", "max", "pool", "pool_avg", "pool_max",
}


class AutoDrain:

    def __init__(self, eng):
        self._e = eng

    def __getattr__(self, name):
        a = getattr(self._e, name)
        if name in _DRAIN_OPS and callable(a):
            def wrapped(*args, **kw):
                r = a(*args, **kw)
                self._e.drain()
                return r
            return wrapped
        return a


class PB:
    def __init__(self, nc):
        self.nc = nc
        self.ops = {e: [] for e in ENGINES}
        self.root = ExitStack()
        self.sems = {}
        self.cnt = {}
        self.base = {}
        self.pool = []
        self._sid = 0

    def sem(self, name):
        if name not in self.sems:
            if self.pool:
                h, tot = self.pool.pop()
            else:
                self._sid += 1
                h = self.root.enter_context(self.nc.semaphore(f"ps{self._sid}"))
                tot = 0
            self.sems[name] = h
            self.base[name] = tot
            self.cnt[name] = 0
        return self.sems[name]

    def end_phase(self, *prefixes):
        names = [n for n in list(self.sems) if any(n.startswith(p) for p in prefixes)]
        for n in names:
            self.wait("gpsimd", n, at=self.cnt[n])
        self.barrier()
        for n in names:
            h = self.sems.pop(n)
            self.pool.append((h, self.base.pop(n) + self.cnt.pop(n)))

    def sb(self, name, shape, dtype, es=None):
        return (es or self.root).enter_context(self.nc.sbuf_tensor(name, shape, dtype))

    def ps(self, name, shape, dtype=F32, es=None):
        return (es or self.root).enter_context(self.nc.psum_tensor(name, shape, dtype))

    def on(self, eng, fn):
        self.ops[eng].append(fn)

    def wait(self, eng, sem_name, at):
        s = self.sem(sem_name)
        b = self.base[sem_name]
        if at > 0:
            self.on(eng, lambda e, s=s, v=b + at: e.wait_ge(s, v))

    def inc(self, eng, sem_name, n=1):
        s = self.sem(sem_name)
        self.cnt[sem_name] += n
        self.on(eng, lambda e, s=s, n=n: e.drain().then_inc(s, n))
        return self.cnt[sem_name]

    def dma(self, eng, out, in_, sem_name, **kw):
        s = self.sem(sem_name)
        self.cnt[sem_name] += 16
        self.on(eng, lambda e, o=out, i=in_, s=s, kw=kw:
                e.dma_start(out=o, in_=i, **kw).then_inc(s, 16))
        return self.cnt[sem_name]

    def barrier(self, *engines):
        engines = ENGINES
        s = self.sem("bar")
        for e in engines:
            self.inc(e, "bar")
        tgt = self.cnt["bar"]
        for e in engines:
            self.wait(e, "bar", at=tgt)

    def finalize(self):
        nc = self.nc
        with nc.Block() as block:
            @block.sync
            def _(e):
                for f in self.ops["sync"]:
                    f(e)

            @block.gpsimd
            def _(e):
                for f in self.ops["gpsimd"]:
                    f(e)

            @block.tensor
            def _(e):
                for f in self.ops["tensor"]:
                    f(e)

            @block.scalar
            def _(e):
                e = AutoDrain(e)
                for f in self.ops["scalar"]:
                    f(e)

            @block.vector
            def _(e):
                e = AutoDrain(e)
                for f in self.ops["vector"]:
                    f(e)
        self.root.close()


class MMPipe:

    def __init__(self, pb, name, banks, evac_eng="vector"):
        self.pb, self.name, self.banks, self.ee = pb, name, banks, evac_eng
        self.i = 0
        self.mm_sem = name + "_mm"
        self.ev_sem = name + "_ev"

    def emit(self, mm_fn, evac_fn):
        pb, nb = self.pb, len(self.banks)
        bank = self.banks[self.i % nb]
        if self.i >= nb:
            pb.wait("tensor", self.ev_sem, at=self.i - nb + 1)
        pb.on("tensor", lambda t, b=bank, f=mm_fn: f(t, b))
        smm = pb.inc("tensor", self.mm_sem)
        pb.wait(self.ee, self.mm_sem, at=smm)
        pb.on(self.ee, lambda e, b=bank, f=evac_fn: f(e, b))
        pb.inc(self.ee, self.ev_sem)
        self.i += 1

    def flush(self):
        self.pb.wait("tensor", self.ev_sem, at=self.i)


def build_program(upto="Z", dbg=()):
    nc = bass.Bass("TRN2", target_bir_lowering=False, debug=False)
    pb = PB(nc)
    dr = {}

    def din(name, shape, dtype=F32):
        dr[name] = nc.dram_tensor(name, list(shape), dtype, kind="ExternalInput").ap()
        return dr[name]

    def dout(name, shape, dtype=F32):
        dr[name] = nc.dram_tensor(name, list(shape), dtype, kind="ExternalOutput").ap()
        return dr[name]

    def dtmp(name, shape, dtype=BF16):
        dr[name] = nc.dram_tensor(name, list(shape), dtype, kind="Internal").ap()
        return dr[name]

    x_in = din("x", (S, D))
    wq_in = din("wq", (D, D))
    wk_in = din("wk", (D, D))
    wv_in = din("wv", (D, D))
    wo_in = din("wo", (D, D))
    wg_in = din("wg", (I_FFN, D))
    wu_in = din("wu", (I_FFN, D))
    wd_in = din("wd", (D, I_FFN))
    cid_in = din("cident", (128, 128))
    clt_in = din("clt", (128, 128))
    cio_in = din("ciota", (128, 128))
    cp8_in = din("cp8", (8 * 128, 128))
    choff_in = din("half_off", (1, 1))
    hoffi_in = din("hoffi", (1, 2), mybir.dt.int32)

    out = dout("out", (SH, D))
    taps = {}

    def tap(name, shape, dtype=F32):
        if name in dbg:
            taps[name] = dout("dbg_" + name, shape, dtype)
        return taps.get(name)

    v_spill = dtmp("v_spill", (S, D), BF16)
    kt_spill = dtmp("kt_spill", (8 * 128, S), BF16)
    xnq_spill = dtmp("xnq_spill", (8 * 128, SH), BF16)
    ffn_spill = dtmp("ffn_spill", (SH, D), BF16)
    wgt_spill = dtmp("wgt_spill", (D, I_FFN), BF16)
    wut_spill = dtmp("wut_spill", (D, I_FFN), BF16)
    wdt_spill = dtmp("wdt_spill", (I_FFN, D), BF16)
    knat_spill = dtmp("knat_spill", (S, D), BF16)
    cc_in_t = nc.dram_tensor("cc_in", [1, S], F32, kind="Internal")
    cc_out_t = nc.dram_tensor("cc_out", [1, S], F32, kind="Internal")
    cc_in = cc_in_t.ap()
    cc_out = cc_out_t.ap()

    ident = pb.sb("ident", [128, 128], BF16)
    ltb = pb.sb("ltb", [128, 128], BF16)
    iotab = pb.sb("iotab", [128, 128], F32)
    p8b = pb.sb("p8b", [128, 8 * 128], BF16)
    hoff = pb.sb("hoff", [1, 1], F32)
    qT = pb.sb("qT", [128, 4 * S + 4 * SH], BF16)
    impP = pb.sb("impP", [128, 32], F32)

    def qT_pair(p):
        if p < 4:
            return qT[:, p * S:(p + 1) * S]
        return qT[:, 4 * S + (p - 4) * SH: 4 * S + (p - 3) * SH]

    psA = pb.ps("psA", [128, 512])
    psB = pb.ps("psB", [128, 512])
    psBIG = pb.ps("psBIG", [128, 2048])
    psC = psBIG[:, 0:512]
    psD = psBIG[:, 512:1024]
    esPS = ExitStack()
    pbt = pb.ps("pbt", [128, 1024], BF16, esPS)
    pbt2 = pb.ps("pbt2", [128, 1024], BF16, esPS)

    pb.dma("gpsimd", ident[:], cid_in, "ld")
    pb.dma("gpsimd", ltb[:], clt_in, "ld")
    pb.dma("sync", iotab[:], cio_in, "ld")
    pb.dma("gpsimd", p8b[:].rearrange("p (c g) -> p c g", c=8), cp8_in.rearrange("(c p) g -> p c g", p=128), "ld")
    pb.dma("sync", hoff[:], choff_in, "ld")
    consts_at = pb.cnt["ld"]
    for e in ("tensor", "vector", "scalar"):
        pb.wait(e, "ld", at=consts_at)

    esXn = ExitStack()
    xnT = pb.sb("xnT", [128, 8 * S], BF16, esXn)
    esA = ExitStack()
    xbuf = [pb.sb(f"xa{i}", [128, D], F32, esA) for i in range(2)]
    xnb = [pb.sb(f"xn32_{i}", [128, D], F32, esA) for i in range(2)]
    sqb = pb.sb("sqb", [128, D], F32, esA)
    st4 = pb.sb("st4", [128, 12], F32, esA)
    xn16 = pb.sb("xn16", [128, D], BF16, esA)
    q32 = pb.sb("q32", [128, D], F32, esA)
    qy = pb.sb("qy", [128, D], F32, esA)
    qy2 = pb.sb("qy2", [128, D], F32, esA)
    qy3 = pb.sb("qy3", [128, D], F32, esA)
    qsc = pb.sb("qsc", [128, D // 16], F32, esA)
    qrc = pb.sb("qrc", [128, D // 16], F32, esA)
    xq16 = pb.sb("xq16", [128, D], BF16, esA)

    for jc in range(32):
        bi = jc % 2
        xc, xn = xbuf[bi], xnb[bi]
        if jc >= 2:
            pb.wait("sync", "a_dn", at=jc - 1)
        lat = pb.dma("sync", xc[:], x_in[jc * 128:(jc + 1) * 128, :], "a_ld")
        pb.wait("vector", "a_ld", at=lat)
        pb.on("vector", lambda v, x=xc: v.reduce_sum(st4[:, 0:1], x[:], axis=AX.X))
        pb.wait("scalar", "a_ld", at=lat)
        if jc >= 1:
            pb.wait("scalar", "a_st", at=jc)
        pb.on("scalar", lambda s, x=xc: s.activation(sqb[:], x[:], AF.Square))
        ssq = pb.inc("scalar", "a_sq")
        pb.wait("vector", "a_sq", at=ssq)
        pb.on("vector", lambda v: v.reduce_sum(st4[:, 1:2], sqb[:], axis=AX.X))

        def stats(v):
            v.tensor_scalar(st4[:, 2:3], st4[:, 0:1], 1.0 / D, None, op0=ALU.mult)
            v.tensor_scalar(st4[:, 3:4], st4[:, 2:3], -1.0, None, op0=ALU.mult)
            v.tensor_tensor(st4[:, 4:5], st4[:, 2:3], st4[:, 2:3], op=ALU.mult)
            v.tensor_scalar(st4[:, 4:5], st4[:, 4:5], -1.0, None, op0=ALU.mult)
            v.tensor_scalar(st4[:, 5:6], st4[:, 1:2], 1.0 / D, None, op0=ALU.mult)
            v.tensor_tensor(st4[:, 4:5], st4[:, 5:6], st4[:, 4:5], op=ALU.add)
            v.tensor_scalar(st4[:, 4:5], st4[:, 4:5], EPS_LN, None, op0=ALU.add)
            v.reciprocal(st4[:, 5:6], st4[:, 4:5])
        pb.on("vector", stats)
        sst = pb.inc("vector", "a_st")
        pb.wait("scalar", "a_st", at=sst)
        pb.on("scalar", lambda s: s.activation(st4[:, 6:7], st4[:, 5:6], AF.Sqrt))
        srs = pb.inc("scalar", "a_rs")
        pb.wait("vector", "a_rs", at=srs)
        pb.on("vector", lambda v: v.tensor_tensor(st4[:, 7:8], st4[:, 3:4], st4[:, 6:7], op=ALU.mult))
        svb = pb.inc("vector", "a_vb")
        pb.wait("scalar", "a_vb", at=svb)
        pb.on("scalar", lambda s, x=xc, xn=xn:
              s.activation(xn[:], x[:], AF.Identity, bias=st4[:, 7:8], scale=st4[:, 6:7]))
        sxn = pb.inc("scalar", "a_xn")
        pb.wait("vector", "a_xn", at=sxn)
        pb.on("vector", lambda v, xn=xn: v.tensor_copy(xn16[:], xn[:]))
        sxc = pb.inc("vector", "a_xc")
        pb.wait("tensor", "a_xc", at=sxc)
        if jc >= 1:
            pb.wait("tensor", "a_te", at=jc)

        def tps(t):
            for mc in range(8):
                t.transpose(pbt[:, mc * 128:(mc + 1) * 128], xn16[:, mc * 128:(mc + 1) * 128], ident[:])
        pb.on("tensor", tps)
        stp = pb.inc("tensor", "a_tp")
        pb.wait("vector", "a_tp", at=stp)

        def evac(v, jc=jc):
            for mc in range(8):
                v.tensor_copy(xnT[:, mc * S + jc * 128: mc * S + (jc + 1) * 128],
                              pbt[:, mc * 128:(mc + 1) * 128])
        pb.on("vector", evac)
        pb.inc("vector", "a_te")

        if jc < 16:
            def qdq1(v, xn=xn):
                xnr = xn.rearrange("p (g w) -> p g w", w=16)
                v.tensor_reduce(qsc[:], xnr, axis=AX.X, op=ALU.max, apply_absolute_value=True)
                v.tensor_scalar(qsc[:], qsc[:], 1.0 / 6.0, 1e-12, op0=ALU.mult, op1=ALU.max)
                v.reciprocal(qrc[:], qsc[:])
                v.tensor_tensor(q32.rearrange("p (g w) -> p g w", w=16), xnr,
                                qrc[:].to_broadcast([128, 64, 16]), op=ALU.mult)
            pb.on("vector", qdq1)
            sq1 = pb.inc("vector", "a_q1")
            pb.wait("scalar", "a_q1", at=sq1)
            pb.on("scalar", lambda s: s.activation(qy[:], q32[:], AF.Abs))
            pb.on("scalar", lambda s: s.activation(q32[:], q32[:], AF.Sign))
            sq2 = pb.inc("scalar", "a_q2")
            pb.wait("vector", "a_q2", at=sq2)

            def qdq2(v):
                v.tensor_scalar(qy2[:], qy[:], MAGIC, MAGIC, op0=ALU.add, op1=ALU.subtract)
                v.tensor_scalar(qy2[:], qy2[:], 2.0, 0.0, op0=ALU.subtract, op1=ALU.max)
                v.tensor_scalar(qy2[:], qy2[:], 2.0, 2.0, op0=ALU.min, op1=ALU.mult)
                v.tensor_scalar(qy3[:], qy[:], 0.5, MAGIC, op0=ALU.mult, op1=ALU.add)
                v.tensor_scalar(qy3[:], qy3[:], MAGIC, 2.0, op0=ALU.subtract, op1=ALU.subtract)
                v.tensor_scalar(qy3[:], qy3[:], 0.0, 4.0, op0=ALU.max, op1=ALU.mult)
                v.tensor_scalar(qy[:], qy[:], 2.0, MAGIC, op0=ALU.mult, op1=ALU.add)
                v.tensor_scalar(qy[:], qy[:], MAGIC, 4.0, op0=ALU.subtract, op1=ALU.min)
                v.tensor_tensor(qy[:], qy[:], qy2[:], op=ALU.add)
                v.tensor_tensor(qy[:], qy[:], qy3[:], op=ALU.add)
                v.tensor_scalar(qrc[:], qsc[:], 0.5, None, op0=ALU.mult)
                v.tensor_tensor(qy2.rearrange("p (g w) -> p g w", w=16),
                                q32.rearrange("p (g w) -> p g w", w=16),
                                qrc[:].to_broadcast([128, 64, 16]), op=ALU.mult)
                v.tensor_tensor(qy[:], qy[:], qy2[:], op=ALU.mult)
                v.tensor_copy(xq16[:], qy[:])
            pb.on("vector", qdq2)
            sq3 = pb.inc("vector", "a_q3")
            pb.wait("tensor", "a_q3", at=sq3)
            if jc >= 1:
                pb.wait("tensor", "a_qe", at=jc)

            def tpq(t):
                for mc in range(8):
                    t.transpose(pbt2[:, mc * 128:(mc + 1) * 128], xq16[:, mc * 128:(mc + 1) * 128], ident[:])
            pb.on("tensor", tpq)
            sqt = pb.inc("tensor", "a_qt")
            pb.wait("vector", "a_qt", at=sqt)

            def evq(v):
                for mc in range(8):
                    v.tensor_copy(xq16[:, mc * 128:(mc + 1) * 128], pbt2[:, mc * 128:(mc + 1) * 128])
            pb.on("vector", evq)
            sqe = pb.inc("vector", "a_qe")
            if jc >= 2:
                pb.wait("gpsimd", "a_qe", at=sqe)
            pb.wait("gpsimd", "a_qe", at=sqe)
            pb.dma("gpsimd",
                   xnq_spill.rearrange("(mc p) s -> p mc s", p=128)[:, :, jc * 128:(jc + 1) * 128],
                   xq16.rearrange("p (mc c) -> p mc c", c=128), "a_qs")
        pb.inc("vector", "a_dn")

    tx = tap("xnT", (128, 8 * S))
    txq = tap("xnq", (8 * 128, SH))
    if upto == "A":
        pb.barrier()
        if tx is not None:
            pb.dma("gpsimd", tx, xnT[:], "st")
        if txq is not None:
            pb.wait("gpsimd", "a_qs", at=pb.cnt["a_qs"])
            pb.dma("gpsimd", dr["dbg_xnq"], xnq_spill, "st")
        pb.wait("gpsimd", "st", at=pb.cnt["st"])
        pb.barrier()
        esA.close()
        esXn.close()
        esPS.close()
        pb.finalize()
        return nc, dr

    pb.barrier()
    esA.close()
    pb.end_phase("a_")
    esB = ExitStack()
    wbuf = pb.sb("wbuf", [128, 8 * D], BF16, esB)
    vstage = [pb.sb(f"vst{i}", [128, D], BF16, esB) for i in range(2)]
    ktst = [pb.sb(f"kts{i}", [128, 512], BF16, esB) for i in range(2)]

    def load_w(w_in):
        return pb.dma("gpsimd", wbuf[:].rearrange("p (mc d) -> p mc d", mc=8), w_in.rearrange("(mc p) d -> p mc d", p=128), "b_w")

    pipe = MMPipe(pb, "b", [psA, psB])

    wat = load_w(wq_in)
    pb.wait("tensor", "b_w", at=wat)
    for p in range(NPAIR):
        for tw in (range(8) if p < 4 else range(4)):
            def mmq(t, bank, p=p, tw=tw):
                for mc in range(8):
                    t.matmul(bank[:], wbuf[:, mc * D + p * 128: mc * D + p * 128 + 128],
                             xnT[:, mc * S + tw * 512: mc * S + (tw + 1) * 512],
                             start=(mc == 0), stop=(mc == 7))
            dst = qT_pair(p)[:, tw * 512:(tw + 1) * 512]
            pipe.emit(mmq, lambda v, bank, d=dst: v.tensor_copy(d, bank[:]))
    pipe.flush()
    pb.barrier("tensor", "vector", "gpsimd")

    wat = load_w(wk_in)
    pb.wait("tensor", "b_w", at=wat)
    ks_i = [0]
    for p in range(NPAIR):
        for tw in range(8):
            def mmk(t, bank, p=p, tw=tw):
                for mc in range(8):
                    t.matmul(bank[:], wbuf[:, mc * D + p * 128: mc * D + p * 128 + 128],
                             xnT[:, mc * S + tw * 512: mc * S + (tw + 1) * 512],
                             start=(mc == 0), stop=(mc == 7))
            i = ks_i[0]
            stg = ktst[i % 2]
            def evk(v, bank, stg=stg, i=i):
                v.tensor_copy(stg[:], bank[:])
            if i >= 2:
                pb.wait("vector", "b_ks", at=16 * (i - 1))
            pipe.emit(mmk, evk)
            pb.wait("gpsimd", pipe.ev_sem, at=pipe.i)
            pb.dma("gpsimd", kt_spill[p * 128:(p + 1) * 128, tw * 512:(tw + 1) * 512],
                   stg[:], "b_ks")
            ks_i[0] += 1
    pipe.flush()
    pb.barrier("tensor", "vector", "gpsimd")

    wat = load_w(wv_in)
    pb.wait("tensor", "b_w", at=wat)
    for jc in range(32):
        stg = vstage[jc % 2]
        if jc >= 2:
            pb.wait("vector", "b_vs", at=16 * (jc - 1))
        for dh in range(2):
            def mmv(t, bank, jc=jc, dh=dh):
                for mc in range(8):
                    t.matmul(bank[:], xnT[:, mc * S + jc * 128: mc * S + (jc + 1) * 128],
                             wbuf[:, mc * D + dh * 512: mc * D + (dh + 1) * 512],
                             start=(mc == 0), stop=(mc == 7))
            pipe.emit(mmv, lambda v, bank, stg=stg, dh=dh:
                      v.tensor_copy(stg[:, dh * 512:(dh + 1) * 512], bank[:]))
        pb.wait("gpsimd", pipe.ev_sem, at=pipe.i)
        pb.dma("gpsimd", v_spill[jc * 128:(jc + 1) * 128, :], stg[:], "b_vs")
    pipe.flush()
    pb.barrier()
    esB.close()
    esXn.close()
    pb.end_phase("b_")

    tq = tap("qT", (128, 4 * S + 4 * SH))
    tk = None
    tv = tap("v", (S, D))
    tkt = tap("ktsp", (8 * 128, S))
    txn2 = tap("xnT2", (128, 8 * S))
    if upto == "B":
        pb.sem("st")
        if txn2 is not None:
            pb.dma("gpsimd", txn2, xnT[:], "st")
        if tq is not None:
            pb.dma("gpsimd", tq, qT[:], "st")
        if tv is not None:
            pb.dma("gpsimd", tv, v_spill, "st")
        if tkt is not None:
            pb.dma("gpsimd", tkt, kt_spill, "st")
        pb.wait("gpsimd", "st", at=pb.cnt.get("st", 0))
        pb.barrier()
        esPS.close()
        pb.finalize()
        return nc, dr

    esF0 = ExitStack()
    fw32 = [pb.sb(f"fw32_{i}", [128, I_FFN], F32, esF0) for i in range(2)]
    gy = pb.sb("gy", [128, I_FFN], F32, esF0)
    gy2 = pb.sb("gy2", [128, I_FFN], F32, esF0)
    gy3 = pb.sb("gy3", [128, I_FFN], F32, esF0)
    gsg = pb.sb("gsg", [128, I_FFN], F32, esF0)
    gsc = pb.sb("gsc", [128, I_FFN // 16], F32, esF0)
    grc = pb.sb("grc", [128, I_FFN // 16], F32, esF0)
    gw16 = pb.sb("gw16", [128, I_FFN], BF16, esF0)
    gstage = pb.sb("gstage", [128, I_FFN], BF16, esF0)

    def qdq_w(v, w):
        nb = w // 16
        sr = gsg[:, 0:w].rearrange("p (g w) -> p g w", w=16)
        v.tensor_reduce(gsc[:, 0:nb], sr, axis=AX.X, op=ALU.max, apply_absolute_value=True)
        v.tensor_scalar(gsc[:, 0:nb], gsc[:, 0:nb], 1.0 / 6.0, 1e-12, op0=ALU.mult, op1=ALU.max)
        v.reciprocal(grc[:, 0:nb], gsc[:, 0:nb])
        v.tensor_tensor(gsg[:, 0:w].rearrange("p (g w) -> p g w", w=16), sr,
                        grc[:, 0:nb].to_broadcast([128, nb, 16]), op=ALU.mult)

    def qdq_w2(v, w):
        nb = w // 16
        v.tensor_scalar(gy2[:, 0:w], gy[:, 0:w], MAGIC, MAGIC, op0=ALU.add, op1=ALU.subtract)
        v.tensor_scalar(gy2[:, 0:w], gy2[:, 0:w], 2.0, 0.0, op0=ALU.subtract, op1=ALU.max)
        v.tensor_scalar(gy2[:, 0:w], gy2[:, 0:w], 2.0, 2.0, op0=ALU.min, op1=ALU.mult)
        v.tensor_scalar(gy3[:, 0:w], gy[:, 0:w], 0.5, MAGIC, op0=ALU.mult, op1=ALU.add)
        v.tensor_scalar(gy3[:, 0:w], gy3[:, 0:w], MAGIC, 2.0, op0=ALU.subtract, op1=ALU.subtract)
        v.tensor_scalar(gy3[:, 0:w], gy3[:, 0:w], 0.0, 4.0, op0=ALU.max, op1=ALU.mult)
        v.tensor_scalar(gy[:, 0:w], gy[:, 0:w], 2.0, MAGIC, op0=ALU.mult, op1=ALU.add)
        v.tensor_scalar(gy[:, 0:w], gy[:, 0:w], MAGIC, 4.0, op0=ALU.subtract, op1=ALU.min)
        v.tensor_tensor(gy[:, 0:w], gy[:, 0:w], gy2[:, 0:w], op=ALU.add)
        v.tensor_tensor(gy[:, 0:w], gy[:, 0:w], gy3[:, 0:w], op=ALU.add)
        v.tensor_scalar(grc[:, 0:nb], gsc[:, 0:nb], 0.5, None, op0=ALU.mult)
        v.tensor_tensor(gy2[:, 0:w].rearrange("p (g w) -> p g w", w=16),
                        gsg[:, 0:w].rearrange("p (g w) -> p g w", w=16),
                        grc[:, 0:nb].to_broadcast([128, nb, 16]), op=ALU.mult)
        v.tensor_tensor(gy[:, 0:w], gy[:, 0:w], gy2[:, 0:w], op=ALU.mult)
        v.tensor_copy(gw16[:, 0:w], gy[:, 0:w])

    nw0 = [0]
    for (w_in2, sp, nch, wid) in ((wg_in, wgt_spill, 22, D), (wu_in, wut_spill, 22, D),
                                  (wd_in, wdt_spill, 8, I_FFN)):
        for ch in range(nch):
            i0 = nw0[0]
            buf = fw32[i0 % 2]
            if i0 >= 2:
                pb.wait("sync", "g_q1", at=pb.cnt.get("g_q1", 0) - 1)
            lat = pb.dma("sync", buf[:, 0:wid], w_in2[ch * 128:(ch + 1) * 128, :], "g_ld")
            pb.wait("vector", "g_ld", at=lat)
            pb.on("vector", lambda v, b=buf, wid=wid: v.tensor_copy(gsg[:, 0:wid], b[:, 0:wid]))
            pb.on("vector", lambda v, wid=wid: qdq_w(v, wid))
            sq1 = pb.inc("vector", "g_q1")
            pb.wait("scalar", "g_q1", at=sq1)
            pb.on("scalar", lambda s, wid=wid: s.activation(gy[:, 0:wid], gsg[:, 0:wid], AF.Abs))
            pb.on("scalar", lambda s, wid=wid: s.activation(gsg[:, 0:wid], gsg[:, 0:wid], AF.Sign))
            sq2 = pb.inc("scalar", "g_q2")
            pb.wait("vector", "g_q2", at=sq2)
            pb.on("vector", lambda v, wid=wid: qdq_w2(v, wid))
            sq3 = pb.inc("vector", "g_q3")
            nt = wid // 128
            pb.wait("tensor", "g_q3", at=sq3)
            for bt in range(0, nt, 8):
                if nw0[0] > 0 or bt > 0:
                    pb.wait("tensor", "g_te", at=pb.cnt.get("g_te", 0))
                def tpw(t, bt=bt, nt=nt):
                    for c in range(bt, min(bt + 8, nt)):
                        t.transpose(pbt[:, (c - bt) * 128:(c - bt + 1) * 128],
                                    gw16[:, c * 128:(c + 1) * 128], ident[:])
                pb.on("tensor", tpw)
                stw = pb.inc("tensor", "g_tp")
                pb.wait("vector", "g_tp", at=stw)
                def evw(v, bt=bt, nt=nt):
                    for c in range(bt, min(bt + 8, nt)):
                        v.tensor_copy(gstage[:, c * 128:(c + 1) * 128], pbt[:, (c - bt) * 128:(c - bt + 1) * 128])
                pb.on("vector", evw)
                pb.inc("vector", "g_te")
            sev = pb.cnt["g_te"]
            pb.wait("gpsimd", "g_te", at=sev)
            if i0 >= 1:
                pass
            pb.dma("gpsimd", sp.rearrange("(mc p) n -> p mc n", p=128)[:, :, ch * 128:(ch + 1) * 128],
                   gstage[:, 0:wid].rearrange("p (mc c) -> p mc c", c=128), "g_sp")
            nw0[0] += 1
    pb.barrier()
    esF0.close()
    pb.end_phase("g_")

    esF = ExitStack()
    NWS = [512] * 5 + [256]
    NWOFF = [0, 512, 1024, 1536, 2048, 2560]
    wgw = pb.sb("wgw", [128, 8 * 512], BF16, esF)
    wuw = pb.sb("wuw", [128, 8 * 512], BF16, esF)
    wdw = pb.sb("wdw", [128, 22 * 512], BF16, esF)
    hqT8 = pb.sb("hqT8", [128, 8 * 2816], BF16, esF)
    xqst8 = pb.sb("xqst8", [128, 8 * 1024], BF16, esF)
    silu32 = pb.sb("silu32", [128, 512], F32, esF)
    h32 = pb.sb("h32", [128, 512], F32, esF)
    fy = pb.sb("fy", [128, 512], F32, esF)
    fy2 = pb.sb("fy2", [128, 512], F32, esF)
    fy3 = pb.sb("fy3", [128, 512], F32, esF)
    fsg = pb.sb("fsg", [128, 512], F32, esF)
    fsc = pb.sb("fsc", [128, 32], F32, esF)
    frc = pb.sb("frc", [128, 32], F32, esF)
    hq16 = pb.sb("hq16", [128, 512], BF16, esF)
    fstage = [pb.sb(f"fst{i}", [128, 512], BF16, esF) for i in range(2)]

    def qdq_ffn1(v, w):
        nb = w // 16
        sr = h32[:, 0:w].rearrange("p (g w) -> p g w", w=16)
        v.tensor_reduce(fsc[:, 0:nb], sr, axis=AX.X, op=ALU.max, apply_absolute_value=True)
        v.tensor_scalar(fsc[:, 0:nb], fsc[:, 0:nb], 1.0 / 6.0, 1e-12, op0=ALU.mult, op1=ALU.max)
        v.reciprocal(frc[:, 0:nb], fsc[:, 0:nb])
        v.tensor_tensor(fsg[:, 0:w].rearrange("p (g w) -> p g w", w=16), sr,
                        frc[:, 0:nb].to_broadcast([128, nb, 16]), op=ALU.mult)

    def qdq_ffn2(v, w):
        nb = w // 16
        v.tensor_scalar(fy2[:, 0:w], fy[:, 0:w], MAGIC, MAGIC, op0=ALU.add, op1=ALU.subtract)
        v.tensor_scalar(fy2[:, 0:w], fy2[:, 0:w], 2.0, 0.0, op0=ALU.subtract, op1=ALU.max)
        v.tensor_scalar(fy2[:, 0:w], fy2[:, 0:w], 2.0, 2.0, op0=ALU.min, op1=ALU.mult)
        v.tensor_scalar(fy3[:, 0:w], fy[:, 0:w], 0.5, MAGIC, op0=ALU.mult, op1=ALU.add)
        v.tensor_scalar(fy3[:, 0:w], fy3[:, 0:w], MAGIC, 2.0, op0=ALU.subtract, op1=ALU.subtract)
        v.tensor_scalar(fy3[:, 0:w], fy3[:, 0:w], 0.0, 4.0, op0=ALU.max, op1=ALU.mult)
        v.tensor_scalar(fy[:, 0:w], fy[:, 0:w], 2.0, MAGIC, op0=ALU.mult, op1=ALU.add)
        v.tensor_scalar(fy[:, 0:w], fy[:, 0:w], MAGIC, 4.0, op0=ALU.subtract, op1=ALU.min)
        v.tensor_tensor(fy[:, 0:w], fy[:, 0:w], fy2[:, 0:w], op=ALU.add)
        v.tensor_tensor(fy[:, 0:w], fy[:, 0:w], fy3[:, 0:w], op=ALU.add)
        v.tensor_scalar(frc[:, 0:nb], fsc[:, 0:nb], 0.5, None, op0=ALU.mult)
        v.tensor_tensor(fy2[:, 0:w].rearrange("p (g w) -> p g w", w=16),
                        fsg[:, 0:w].rearrange("p (g w) -> p g w", w=16),
                        frc[:, 0:nb].to_broadcast([128, nb, 16]), op=ALU.mult)
        v.tensor_tensor(fy[:, 0:w], fy[:, 0:w], fy2[:, 0:w], op=ALU.mult)
        v.tensor_copy(hq16[:, 0:w], fy[:, 0:w])

    nf = [0]
    for ob in range(2):
        pb.wait("sync", "f_mm", at=pb.cnt.get("f_mm", 0))
        xat = pb.dma("sync", xqst8[:].rearrange("p (mc c) -> p mc c", mc=8),
                     xnq_spill.rearrange("(mc p) s -> p mc s", p=128)[:, :, ob * 1024:(ob + 1) * 1024],
                     "f_x")
        for nwi in range(6):
            w = NWS[nwi]
            pb.wait("gpsimd", "f_mm", at=pb.cnt.get("f_mm", 0))
            pb.dma("gpsimd", wgw[:, 0:8 * w].rearrange("p (mc w) -> p mc w", mc=8),
                   wgt_spill.rearrange("(mc p) n -> p mc n", p=128)[:, :, NWOFF[nwi]:NWOFF[nwi] + w], "f_w")
            wat = pb.dma("gpsimd", wuw[:, 0:8 * w].rearrange("p (mc w) -> p mc w", mc=8),
                         wut_spill.rearrange("(mc p) n -> p mc n", p=128)[:, :, NWOFF[nwi]:NWOFF[nwi] + w], "f_w")
            pb.wait("tensor", "f_w", at=wat)
            pb.wait("tensor", "f_x", at=xat)
            for jc8 in range(8):
                def mmg(t, jc8=jc8, w=w):
                    for mc in range(8):
                        t.matmul(psA[:, 0:w], xqst8[:, mc * 1024 + jc8 * 128: mc * 1024 + (jc8 + 1) * 128],
                                 wgw[:, mc * w:(mc + 1) * w], start=(mc == 0), stop=(mc == 7))
                    for mc in range(8):
                        t.matmul(psB[:, 0:w], xqst8[:, mc * 1024 + jc8 * 128: mc * 1024 + (jc8 + 1) * 128],
                                 wuw[:, mc * w:(mc + 1) * w], start=(mc == 0), stop=(mc == 7))
                if nf[0] >= 1:
                    pb.wait("tensor", "f_hq", at=pb.cnt.get("f_hq", 0))
                pb.on("tensor", mmg)
                smm = pb.inc("tensor", "f_mm")
                pb.wait("scalar", "f_mm", at=smm)
                pb.on("scalar", lambda s, w=w: s.activation(silu32[:, 0:w], psA[:, 0:w], AF.Silu))
                ssl = pb.inc("scalar", "f_sl")
                pb.wait("vector", "f_sl", at=ssl)
                pb.on("vector", lambda v, w=w: v.tensor_tensor(h32[:, 0:w], silu32[:, 0:w], psB[:, 0:w], op=ALU.mult))
                pb.on("vector", lambda v, w=w: qdq_ffn1(v, w))
                sv1 = pb.inc("vector", "f_q1")
                pb.wait("scalar", "f_q1", at=sv1)
                pb.on("scalar", lambda s, w=w: s.activation(fy[:, 0:w], fsg[:, 0:w], AF.Abs))
                pb.on("scalar", lambda s, w=w: s.activation(fsg[:, 0:w], fsg[:, 0:w], AF.Sign))
                ss2 = pb.inc("scalar", "f_q2")
                pb.wait("vector", "f_q2", at=ss2)
                pb.on("vector", lambda v, w=w: qdq_ffn2(v, w))
                sv2 = pb.inc("vector", "f_q3")
                pb.wait("tensor", "f_q3", at=sv2)
                if nf[0] >= 1:
                    pb.wait("tensor", "f_te", at=pb.cnt.get("f_te", 0))
                def tpf(t, w=w):
                    for c in range(w // 128):
                        t.transpose(pbt[:, c * 128:(c + 1) * 128], hq16[:, c * 128:(c + 1) * 128], ident[:])
                pb.on("tensor", tpf)
                stf = pb.inc("tensor", "f_tp")
                pb.wait("vector", "f_tp", at=stf)
                def evf(v, jc8=jc8, nwi=nwi, w=w):
                    for c in range(w // 128):
                        nc2 = NWOFF[nwi] // 128 + c
                        v.tensor_copy(hqT8[:, jc8 * 2816 + nc2 * 128: jc8 * 2816 + (nc2 + 1) * 128],
                                      pbt[:, c * 128:(c + 1) * 128])
                pb.on("vector", evf)
                pb.inc("vector", "f_te")
                pb.inc("vector", "f_hq")
                nf[0] += 1
        for mw in range(2):
            pb.wait("gpsimd", "f_dmm", at=pb.cnt.get("f_dmm", 0))
            dat = pb.dma("gpsimd", wdw[:].rearrange("p (nc w) -> p nc w", nc=22),
                         wdt_spill.rearrange("(nc p) m -> p nc m", p=128)[:, :, mw * 512:(mw + 1) * 512], "f_wd")
            pb.wait("tensor", "f_wd", at=dat)
            for jc8 in range(8):
                jc = ob * 8 + jc8
                bank = psC if jc8 % 2 == 0 else psD
                def mmd(t, jc8=jc8, bank=bank):
                    for nc2 in range(22):
                        t.matmul(bank, hqT8[:, jc8 * 2816 + nc2 * 128: jc8 * 2816 + (nc2 + 1) * 128],
                                 wdw[:, nc2 * 512:(nc2 + 1) * 512], start=(nc2 == 0), stop=(nc2 == 21))
                if jc8 >= 2:
                    pb.wait("tensor", "f_de", at=pb.cnt.get("f_de", 0) - 1)
                pb.on("tensor", mmd)
                sdm = pb.inc("tensor", "f_dmm")
                pb.wait("vector", "f_dmm", at=sdm)
                stg = fstage[jc8 % 2]
                if jc8 >= 2:
                    pb.wait("vector", "f_fs", at=pb.cnt.get("f_fs", 0) - 16)
                pb.on("vector", lambda v, sg=stg, b=bank: v.tensor_copy(sg[:], b))
                sde = pb.inc("vector", "f_de")
                pb.wait("gpsimd", "f_de", at=sde)
                pb.dma("gpsimd", ffn_spill[jc * 128:(jc + 1) * 128, mw * 512:(mw + 1) * 512], stg[:], "f_fs")
    pb.barrier()
    esF.close()
    pb.end_phase("f_")

    tff = tap("ffn", (SH, D))
    if upto == "F":
        pb.sem("st")
        if tff is not None:
            pb.dma("gpsimd", tff, ffn_spill, "st")
        pb.wait("gpsimd", "st", at=pb.cnt.get("st", 0))
        pb.barrier()
        esPS.close()
        pb.finalize()
        return nc, dr

    esP = ExitStack()
    onesb = pb.sb("onesb", [128, 128], BF16, esP)
    onesf = pb.sb("onesf", [128, 128], F32, esP)
    idf32 = pb.sb("idf32", [128, 128], F32, esP)
    iotap = pb.sb("iotap", [128, 2], F32, esP)
    iota512 = pb.sb("iota512", [128, 512], F32, esP)
    hoffs = pb.sb("hoffs", [1, 2], mybir.dt.int32, esP)
    esPt = ExitStack()
    kTp = pb.sb("kTp", [128, S], BF16, esPt)
    knat1 = pb.sb("knat1", [128, 32 * 130], BF16, esPt)
    pTall = pb.sb("pTall", [128, 2 * 32 * 512], BF16, esPt)
    wsb = pb.sb("wsb", [128, 1024], F32, esPt)
    wstk16 = pb.sb("wstk16", [128, 512], BF16, esPt)
    wbmv = pb.sb("wbmv", [128, 512], BF16, esPt)
    zrow = pb.sb("zrow", [128, 1024], F32, esPt)
    rz32 = pb.sb("rz32", [128, 1024], F32, esPt)
    rz16 = pb.sb("rz16", [128, 1024], BF16, esPt)
    rzst16 = pb.sb("rzst16", [128, 512], BF16, esPt)
    rzbA = pb.sb("rzbA", [128, 512], BF16, esPt)
    rzbB = pb.sb("rzbB", [128, 512], BF16, esPt)
    tmul16 = pb.sb("tmul16", [128, 512], BF16, esPt)
    iscr = pb.sb("iscr", [128, 512], BF16, esPt)
    inflb = pb.sb("inflb", [128, 64], F32, esPt)
    infl2 = pb.sb("infl2", [128, 32], F32, esPt)
    inflacc = pb.sb("inflacc", [128, 32], F32, esPt)
    mxacc = pb.sb("mxacc", [128, 1024], BF16, esPt)
    mxq = pb.sb("mxq", [128, 8], BF16, esPt)
    mxt8 = pb.sb("mxt8", [128, 128], BF16, esPt)
    mxrowr = pb.sb("mxrowr", [128, 1024], F32, esPt)
    lnzr = pb.sb("lnzr", [128, 1024], F32, esPt)
    tpr = pb.sb("tpr", [128, 1024], F32, esPt)
    entrow = pb.sb("entrow", [128, S], F32, esPt)
    dbgma = pb.sb("dbgma", [128, 2048], BF16, esPt)
    marow = pb.sb("marow", [128, S], F32, esPt)

    pb.on("gpsimd", lambda g: g.iota(iotap[:, 0:1], [[1, 1]], base=0, channel_multiplier=1,
                                     allow_small_or_imprecise_dtypes=True))
    pb.on("gpsimd", lambda g: g.iota(iota512[:], [[1, 512]], base=0, channel_multiplier=0,
                                     allow_small_or_imprecise_dtypes=True))
    pb.inc("gpsimd", "p_io")
    pb.on("vector", lambda v: v.memset(onesb[:], 1.0))
    pb.on("vector", lambda v: v.memset(onesf[:], 1.0))
    pb.on("vector", lambda v: v.tensor_copy(idf32[:], ident[:]))
    pb.on("vector", lambda v: v.memzero(inflacc[:]))
    pb.on("vector", lambda v: v.memzero(entrow[64:65, :]))
    pb.on("vector", lambda v: v.memzero(marow[64:65, :]))
    pb.inc("vector", "p_c0")
    pb.barrier()

    tap_wsb = tap("wsb", (128, 1024))
    tap_pt = tap("pT", (128, 2 * 32 * 512))
    for p4 in range(4):
        if p4 > 0:
            pb.wait("sync", "p_dn", at=8 * p4)
        klat = pb.dma("sync", kTp[:], kt_spill[p4 * 128:(p4 + 1) * 128, :], "p_kl")
        pb.wait("vector", "p_kl", at=klat)
        kn3 = knat1[:].rearrange("p (c x) -> p c x", x=130)
        pb.on("vector", lambda v: v.memset(kn3[:, :, 64:65], 1.0))
        pb.on("vector", lambda v: v.memset(kn3[:, :, 129:130], 1.0))
        pb.inc("vector", "p_m1")
        pb.wait("tensor", "p_kl", at=klat)
        pb.wait("tensor", "p_m1", at=pb.cnt["p_m1"])
        for g8 in range(4):
            if p4 + g8 > 0:
                pb.wait("tensor", "p_ke", at=pb.cnt.get("p_ke", 0))
            pb.wait("tensor", "p_mq", at=pb.cnt.get("p_mq", 0))
            def tpk(t, g8=g8):
                for s2 in range(8):
                    kc = g8 * 8 + s2
                    t.transpose(pbt[:, s2 * 128:(s2 + 1) * 128],
                                kTp[:, kc * 128:(kc + 1) * 128], ident[:])
            pb.on("tensor", tpk)
            stk = pb.inc("tensor", "p_kt")
            pb.wait("vector", "p_kt", at=stk)
            def evk(v, g8=g8):
                for s2 in range(8):
                    kc = g8 * 8 + s2
                    dst = bass.AP(knat1[:].tensor, kc * 130, [[32 * 130, 128], [65, 2], [1, 64]])
                    v.tensor_copy(dst, pbt[:, s2 * 128:(s2 + 1) * 128].rearrange("p (h d) -> p h d", h=2))
            pb.on("vector", evk)
            pb.inc("vector", "p_ke")

        for qw in range(8):
            for kc2 in range(16):
                pb.wait("tensor", "p_ex", at=pb.cnt.get("p_ex", 0))
                def qk(t, p4=p4, qw=qw, kc2=kc2):
                    for sub in range(2):
                        kc = kc2 * 2 + sub
                        t.matmul(psBIG[:, sub * 512:(sub + 1) * 512],
                                 kTp[0:64, kc * 128:(kc + 1) * 128],
                                 qT[0:64, p4 * S + qw * 512: p4 * S + (qw + 1) * 512],
                                 start=True, stop=True, tile_position=(0, 0))
                        t.matmul(psBIG[:, 1024 + sub * 512: 1536 + sub * 512],
                                 kTp[64:128, kc * 128:(kc + 1) * 128],
                                 qT[64:128, p4 * S + qw * 512: p4 * S + (qw + 1) * 512],
                                 start=True, stop=True, tile_position=(64, 0))
                pb.on("tensor", qk)
                sqk = pb.inc("tensor", "p_qk")
                pb.wait("scalar", "p_qk", at=sqk)
                def ex(s, kc2=kc2):
                    outap = bass.AP(pTall[:].tensor, kc2 * 2 * 512,
                                    [[2 * 32 * 512, 128], [32 * 512, 2], [512, 2], [1, 512]])
                    s.activation(outap, psBIG[:], AF.Exp, scale=SCALE)
                pb.on("scalar", ex)
                pb.inc("scalar", "p_ex")
            pb.wait("tensor", "p_ex", at=pb.cnt["p_ex"])
            if (p4, qw) != (0, 0):
                pb.wait("tensor", "p_we", at=pb.cnt.get("p_we", 0))
            def wmm(t):
                for h in range(2):
                    dst = psA if h == 0 else psB
                    for kc in range(32):
                        t.matmul(dst[0:65, :],
                                 knat1[:, kc * 130 + h * 65: kc * 130 + h * 65 + 65],
                                 pTall[:, h * 16384 + kc * 512: h * 16384 + (kc + 1) * 512],
                                 start=(kc == 0), stop=(kc == 31))
            pb.on("tensor", wmm)
            swm = pb.inc("tensor", "p_wm")
            pb.wait("vector", "p_wm", at=swm)
            pb.on("vector", lambda v: v.tensor_copy(wsb[0:65, 0:512], psA[0:65, :]))
            pb.on("vector", lambda v: v.tensor_copy(wsb[0:65, 512:1024], psB[0:65, :]))
            pb.on("vector", lambda v: v.tensor_copy(wstk16[0:64, :], wsb[0:64, 0:512]))
            pb.on("vector", lambda v: v.tensor_copy(wbmv[0:64, :], wsb[0:64, 512:1024]))
            pb.on("vector", lambda v: v.tensor_copy(zrow[64:65, 0:512], psA[64:65, :]))
            pb.on("vector", lambda v: v.tensor_copy(zrow[64:65, 512:1024], psB[64:65, :]))
            pb.on("vector", lambda v: v.reciprocal(rz32[64:65, :], zrow[64:65, :]))
            pb.on("vector", lambda v: v.tensor_copy(rz16[64:65, :], rz32[64:65, :]))
            swe = pb.inc("vector", "p_we")
            pb.wait("tensor", "p_we", at=swe)
            if (p4, qw) != (0, 0):
                pb.wait("tensor", "p_rbe", at=pb.cnt.get("p_rbe", 0))
            def rzb(t):
                t.matmul(psBIG[64:128, 0:512], ident[0:64, 0:64], wbmv[0:64, :],
                         start=True, stop=True, tile_position=(0, 64))
                t.matmul(psBIG[0:64, 512:1024], onesb[64:65, 0:64], rz16[64:65, 0:512],
                         start=True, stop=True, tile_position=(64, 0))
                t.matmul(psBIG[64:128, 512:1024], onesb[64:65, 64:128], rz16[64:65, 512:1024],
                         start=True, stop=True, tile_position=(64, 64))
                t.matmul(psBIG[:, 1024:1536], onesb[64:65, :], rz16[64:65, 0:512],
                         start=True, stop=True, tile_position=(64, 0))
                t.matmul(psBIG[:, 1536:2048], onesb[64:65, :], rz16[64:65, 512:1024],
                         start=True, stop=True, tile_position=(64, 0))
            pb.on("tensor", rzb)
            srb = pb.inc("tensor", "p_rb")
            pb.wait("vector", "p_rb", at=srb)
            pb.on("vector", lambda v: v.tensor_copy(wstk16[64:128, :], psBIG[64:128, 0:512]))
            pb.on("vector", lambda v: v.tensor_copy(rzst16[:], psBIG[:, 512:1024]))
            pb.on("vector", lambda v: v.tensor_copy(rzbA[:], psBIG[:, 1024:1536]))
            pb.on("vector", lambda v: v.tensor_copy(rzbB[:], psBIG[:, 1536:2048]))
            pb.on("vector", lambda v, p4=p4, qw=qw:
                  v.tensor_tensor(tmul16[:], qT[:, p4 * S + qw * 512: p4 * S + (qw + 1) * 512],
                                  wstk16[:], op=ALU.mult))
            pb.on("vector", lambda v: v.tensor_tensor(tmul16[:], tmul16[:], rzst16[:], op=ALU.mult))
            stm = pb.inc("vector", "p_rbe")
            pb.wait("tensor", "p_rbe", at=stm)
            pb.on("tensor", lambda t: t.matmul(psBIG[64:65, 0:512], onesb[:, 0:1], tmul16[:],
                                               start=True, stop=True, tile_position=(0, 64)))
            stp2 = pb.inc("tensor", "p_tp2")
            def inflf(v):
                for kc in range(32):
                    for h in range(2):
                        v.scalar_tensor_tensor(iscr[:],
                                               pTall[:, h * 16384 + kc * 512: h * 16384 + (kc + 1) * 512],
                                               1.0, rzbA[:] if h == 0 else rzbB[:],
                                               op0=ALU.mult, op1=ALU.mult,
                                               accum_out=inflb[:, kc * 2 + h: kc * 2 + h + 1])
                v.tensor_reduce(infl2[:], inflb[:].rearrange("p (k h) -> p k h", h=2),
                                axis=AX.X, op=ALU.add)
                v.tensor_tensor(inflacc[:], inflacc[:], infl2[:], op=ALU.add)
            pb.on("vector", inflf)
            def mxc(v):
                src0 = bass.AP(pTall[:].tensor, 0, [[2 * 32 * 512, 128], [32 * 512, 2], [1, 512]])
                v.tensor_copy(mxacc[:].rearrange("p (h w) -> p h w", h=2), src0)
                for kc in range(1, 32):
                    srck = bass.AP(pTall[:].tensor, kc * 512, [[2 * 32 * 512, 128], [32 * 512, 2], [1, 512]])
                    v.tensor_tensor(mxacc[:].rearrange("p (h w) -> p h w", h=2),
                                    mxacc[:].rearrange("p (h w) -> p h w", h=2), srck, op=ALU.max)
            pb.on("vector", mxc)
            smx = pb.inc("vector", "p_mx")
            pb.wait("tensor", "p_mx", at=smx)
            if (p4, qw) != (0, 0):
                pb.wait("tensor", "p_mq", at=pb.cnt.get("p_mq", 0))
            def tmx(t):
                for c in range(8):
                    t.transpose(pbt[:, c * 128:(c + 1) * 128], mxacc[:, c * 128:(c + 1) * 128], ident[:])
            pb.on("tensor", tmx)
            stx = pb.inc("tensor", "p_tx")
            pb.wait("vector", "p_tx", at=stx)
            def rmx(v):
                for c in range(8):
                    v.reduce_max(mxq[:, c:c + 1], pbt[:, c * 128:(c + 1) * 128], axis=AX.X)
            pb.on("vector", rmx)
            smq = pb.inc("vector", "p_mq")
            pb.wait("tensor", "p_mq", at=smq)
            pb.on("tensor", lambda t: t.transpose(pbt2[0:8, 0:128], mxq[:, 0:8], ident[:]))
            smt = pb.inc("tensor", "p_mt")
            pb.wait("vector", "p_mt", at=smt)
            pb.on("vector", lambda v: v.tensor_copy(mxt8[0:8, :], pbt2[0:8, 0:128]))
            sme = pb.inc("vector", "p_me")
            pb.wait("gpsimd", "p_me", at=sme)
            mdm = pb.dma("gpsimd", mxrowr[64:65, :].rearrange("p (r c) -> p r c", r=8),
                         mxt8[0:8, :].rearrange("p c -> p 1 c") if False else
                         bass.AP(mxt8[:].tensor, 0, [[128, 8], [0, 1], [1, 128]]), "p_md")
            pb.wait("scalar", "p_we", at=swe)
            pb.on("scalar", lambda s: s.activation(lnzr[64:65, :], zrow[64:65, :], AF.Ln))
            sln = pb.inc("scalar", "p_ln")
            pb.wait("vector", "p_ln", at=sln)
            pb.wait("vector", "p_tp2", at=stp2)
            pb.wait("vector", "p_md", at=mdm)
            def rows(v, qw=qw):
                v.tensor_scalar(tpr[64:65, 0:512], psBIG[64:65, 0:512], SCALE, None, op0=ALU.mult)
                v.tensor_tensor(tpr[64:65, 512:1024], lnzr[64:65, 0:512], lnzr[64:65, 512:1024], op=ALU.add)
                v.tensor_tensor(tpr[64:65, 0:512], tpr[64:65, 512:1024], tpr[64:65, 0:512], op=ALU.subtract)
                v.tensor_tensor(entrow[64:65, qw * 512:(qw + 1) * 512],
                                entrow[64:65, qw * 512:(qw + 1) * 512], tpr[64:65, 0:512], op=ALU.add)
                v.tensor_tensor(tpr[64:65, :], mxrowr[64:65, :], rz32[64:65, :], op=ALU.mult)
                v.tensor_tensor(tpr[64:65, 0:512], tpr[64:65, 0:512], tpr[64:65, 512:1024], op=ALU.add)
                v.tensor_tensor(marow[64:65, qw * 512:(qw + 1) * 512],
                                marow[64:65, qw * 512:(qw + 1) * 512], tpr[64:65, 0:512], op=ALU.add)
            pb.on("vector", rows)
            sdn = pb.inc("vector", "p_dn")
            pb.wait("tensor", "p_dn", at=sdn)
    pb.barrier()
    if tap_wsb is not None:
        pb.dma("gpsimd", tap_wsb, wsb[:], "st")
    if tap_pt is not None:
        pb.dma("gpsimd", tap_pt, pTall[:], "st")
    tw1 = tap("wstk", (128, 512))
    tw2 = tap("rzst", (128, 512))
    tw3 = tap("tmul", (128, 512))
    tw4 = tap("tprd", (1, 1024))
    tw5 = tap("mxrowr", (1, 1024))
    tw6 = tap("mxacc", (128, 1024))
    if tw1 is not None:
        pb.dma("gpsimd", tw1, wstk16[:], "st")
        pb.dma("gpsimd", tw2, rzst16[:], "st")
        pb.dma("gpsimd", tw3, tmul16[:], "st")
        pb.dma("gpsimd", tw4, tpr[64:65, :], "st")
        pb.dma("gpsimd", tw5, mxrowr[64:65, :], "st")
        pb.dma("gpsimd", tap("dbgma2", (128, 2048)), dbgma[:], "st")
        pb.dma("gpsimd", tw6, mxacc[:], "st")
    ti1 = tap("entrow", (1, S))
    ti3 = tap("inflacc", (128, 32))
    if ti1 is not None:
        pb.dma("gpsimd", ti1, entrow[64:65, :], "st")
        pb.dma("gpsimd", tap("marow", (1, S)), marow[64:65, :], "st")
        pb.dma("gpsimd", ti3, inflacc[:], "st")
    if upto == "P":
        pb.sem("st")
        pb.wait("gpsimd", "st", at=pb.cnt.get("st", 0))
        pb.barrier()
        esPt.close()
        esP.close()
        esPS.close()
        pb.finalize()
        return nc, dr

    ihat = pb.dma("sync", hoffs[:], dr["hoffi"], "i_h")

    def comb(v):
        v.tensor_scalar(entrow[64:65, :], entrow[64:65, :], -0.4 / 16.0, None, op0=ALU.mult)
        v.tensor_scalar(marow[64:65, :], marow[64:65, :], 0.3 / 16.0, None, op0=ALU.mult)
        v.tensor_tensor(entrow[64:65, :], entrow[64:65, :], marow[64:65, :], op=ALU.add)
        v.tensor_scalar(inflacc[:], inflacc[:], 0.3 / 16.0, None, op0=ALU.mult)
    pb.on("vector", comb)
    iv0 = pb.inc("vector", "i_v")
    pb.wait("tensor", "i_v", at=iv0)
    pb.on("tensor", lambda t: t.matmul(psA[0:32, 0:128], inflacc[:], idf32[:], start=True, stop=True))
    it1 = pb.inc("tensor", "i_t")
    pb.wait("vector", "i_t", at=it1)
    pb.on("vector", lambda v: v.tensor_copy(wsb[0:32, 0:128], psA[0:32, 0:128]))
    iv1 = pb.inc("vector", "i_v")
    pb.wait("gpsimd", "i_v", at=iv1)
    idm = pb.dma("gpsimd", marow[64:65, :].rearrange("p (c j) -> p c j", c=32),
                 bass.AP(wsb[:].tensor, 0, [[1024, 32], [0, 1], [1, 128]]), "i_d")
    pb.wait("vector", "i_d", at=idm)
    pb.on("vector", lambda v: v.tensor_tensor(entrow[64:65, :], entrow[64:65, :],
                                              marow[64:65, :], op=ALU.add))
    iv2 = pb.inc("vector", "i_v")
    pb.wait("gpsimd", "i_v", at=iv2)
    pb.wait("gpsimd", "i_h", at=ihat)

    cs_sem, cc_sem = pb.sem("i_cs"), pb.sem("i_cc")
    cs_b, cc_b = pb.base["i_cs"], pb.base["i_cc"]

    def cc(g, cs=cs_sem, csb=cs_b, ccs=cc_sem, ccb=cc_b):
        r1 = nc.values_load(hoffs[0:1, 0:1], engines=[mybir.EngineType.Pool],
                            min_val=0, max_val=SH, skip_runtime_bounds_check=True)
        r2 = nc.values_load(hoffs[0:1, 1:2], engines=[mybir.EngineType.Pool],
                            min_val=0, max_val=SH, skip_runtime_bounds_check=True)
        g.dma_start(out=cc_in[0:1, bass.ds(r1, SH)], in_=entrow[64:65, 0:SH]).then_inc(cs, 16)
        g.dma_start(out=cc_in[0:1, bass.ds(r2, SH)], in_=entrow[64:65, SH:S]).then_inc(cs, 16)
        g.wait_ge(cs, csb + 32)
        g.collective_compute("AllReduce", ALU.add,
                             replica_groups=[[0, 1], [2, 3], [4, 5], [6, 7]],
                             ins=[cc_in_t.ap().opt()], outs=[cc_out_t.ap().opt()],
                             ).then_inc(ccs, 1)
        g.wait_ge(ccs, ccb + 1)
    pb.on("gpsimd", cc)
    pb.cnt["i_cs"] = 32
    pb.cnt["i_cc"] = 1
    pb.end_phase("p_")
    esPt.close()
    esPS.close()

    esIk = ExitStack()
    posm = pb.sb("posm", [128, 32], F32, esIk)
    negposm = pb.sb("negposm", [128, 32], F32, esIk)
    valid16 = pb.sb("valid16", [128, 1], F32, esIk)
    esI = ExitStack()
    impb = pb.sb("impb", [128, S], F32, esI)
    cscr = pb.sb("cscr", [128, S], F32, esI)
    istg = pb.sb("istg", [32, 128], F32, esI)
    thrc = pb.sb("thrc", [128, 8], F32, esI)
    cnt1 = pb.sb("cnt1", [128, 8], F32, esI)
    selc = pb.sb("selc", [128, 8], F32, esI)
    maskc = pb.sb("maskc", [128, 32], F32, esI)
    mask16 = pb.sb("mask16", [128, 32], BF16, esI)
    posw = pb.sb("posw", [128, 32], F32, esI)
    rowt = pb.sb("rowt", [1, 160], F32, esI)
    improw = pb.sb("improw", [128, S], F32, esI)

    def ccrd(g, cs=cs_sem):
        r1 = nc.values_load(hoffs[0:1, 0:1], engines=[mybir.EngineType.Pool],
                            min_val=0, max_val=SH, skip_runtime_bounds_check=True)
        r2 = nc.values_load(hoffs[0:1, 1:2], engines=[mybir.EngineType.Pool],
                            min_val=0, max_val=SH, skip_runtime_bounds_check=True)
        g.dma_start(out=improw[64:65, 0:SH], in_=cc_out[0:1, bass.ds(r1, SH)]).then_inc(cs, 16)
        g.dma_start(out=improw[64:65, SH:S], in_=cc_out[0:1, bass.ds(r2, SH)]).then_inc(cs, 16)
    pb.on("gpsimd", ccrd)
    pb.cnt["i_cs"] = 64
    for e in ("vector", "tensor", "gpsimd"):
        pb.wait(e, "i_cs", at=64)
    idm2 = pb.dma("gpsimd", bass.AP(istg[:].tensor, 0, [[128, 32], [0, 1], [1, 128]]),
                  improw[64:65, :].rearrange("p (c j) -> p c j", c=32), "i_d2")
    pb.wait("tensor", "i_d2", at=idm2)
    pb.on("tensor", lambda t: t.matmul(psA[:, 0:32], istg[:], idf32[0:32, 0:32], start=True, stop=True))
    it2 = pb.inc("tensor", "i_t")
    pb.wait("vector", "i_t", at=it2)
    pb.on("vector", lambda v: v.tensor_copy(impP[:], psA[:, 0:32]))
    iv3 = pb.inc("vector", "i_v")
    pb.wait("tensor", "i_v", at=iv3)
    for grp in range(2):
        if grp == 1:
            pb.wait("tensor", "i_ve", at=1)
        def impbm(t, grp=grp):
            for w2 in range(4):
                t.matmul(psBIG[:, w2 * 512:(w2 + 1) * 512], onesf[64:65, :],
                         improw[64:65, (grp * 4 + w2) * 512:(grp * 4 + w2 + 1) * 512],
                         start=True, stop=True, tile_position=(64, 0))
        pb.on("tensor", impbm)
        ig = pb.inc("tensor", "i_tg")
        pb.wait("vector", "i_tg", at=ig)
        pb.on("vector", lambda v, grp=grp: v.tensor_copy(impb[:, grp * 2048:(grp + 1) * 2048], psBIG[:]))
        pb.inc("vector", "i_ve")
    pb.barrier()

    def bnds(v):
        v.reduce_max(cnt1[:, 0:1], impP[:], axis=AX.X)
        v.tensor_scalar(cscr[:, 0:32], impP[:], -1.0, None, op0=ALU.mult)
        v.reduce_max(cnt1[:, 1:2], cscr[:, 0:32], axis=AX.X)
    pb.on("vector", bnds)
    pb.barrier()

    def trb(t):
        t.matmul(psB[0:1, 0:128], cnt1[:, 0:1], idf32[:], start=True, stop=True)
        t.matmul(psB[0:1, 128:256], cnt1[:, 1:2], idf32[:], start=True, stop=True)
    pb.on("tensor", trb)
    pb.barrier()

    def bfin(v):
        v.reduce_max(rowt[0:1, 0:1], psB[0:1, 0:128], axis=AX.X)
        v.reduce_max(rowt[0:1, 3:4], psB[0:1, 128:256], axis=AX.X)
        v.tensor_scalar(rowt[0:1, 1:2], rowt[0:1, 3:4], -1.0, None, op0=ALU.mult)
        v.tensor_scalar(rowt[0:1, 1:2], rowt[0:1, 1:2], -1e-4, None, op0=ALU.add)
        v.tensor_scalar(rowt[0:1, 0:1], rowt[0:1, 0:1], 1e-4, None, op0=ALU.add)
    pb.on("vector", bfin)
    pb.barrier()

    for stage in range(3):
        def grid(v):
            v.tensor_scalar(rowt[0:1, 4:5], rowt[0:1, 1:2], -1.0, None, op0=ALU.mult)
            v.tensor_tensor(rowt[0:1, 2:3], rowt[0:1, 0:1], rowt[0:1, 4:5], op=ALU.add)
            v.tensor_scalar(rowt[0:1, 2:3], rowt[0:1, 2:3], 1.0 / 129.0, None, op0=ALU.mult)
        pb.on("vector", grid)
        pb.barrier()

        def bstep(t):
            t.matmul(psA[:, 0:1], onesf[0:1, :], rowt[0:1, 1:2], start=True, stop=True)
            t.matmul(psA[:, 1:2], onesf[0:1, :], rowt[0:1, 2:3], start=True, stop=True)
        pb.on("tensor", bstep)
        pb.barrier()

        def thrs(v):
            v.tensor_copy(thrc[:, 0:1], psA[:, 0:1])
            v.tensor_copy(thrc[:, 1:2], psA[:, 1:2])
            v.tensor_scalar(cnt1[:, 3:4], iotap[:, 0:1], 1.0, None, op0=ALU.add)
            v.tensor_tensor(thrc[:, 2:3], cnt1[:, 3:4], thrc[:, 1:2], op=ALU.mult)
            v.tensor_tensor(thrc[:, 2:3], thrc[:, 2:3], thrc[:, 0:1], op=ALU.add)
            v.tensor_scalar(thrc[:, 3:4], thrc[:, 2:3], -1.0, None, op0=ALU.mult)
            v.tensor_scalar(cscr[:], impb[:], thrc[:, 3:4], None, op0=ALU.add)
            v.tensor_scalar(cscr[:], cscr[:], 0.0, None, op0=ALU.is_gt)
            v.reduce_sum(cnt1[:, 0:1], cscr[:], axis=AX.X)
            v.tensor_scalar(selc[:, 0:1], cnt1[:, 0:1], -(HH - 0.5), None, op0=ALU.add)
            v.tensor_scalar(selc[:, 0:1], selc[:, 0:1], 0.0, None, op0=ALU.is_gt)
            v.tensor_tensor(selc[:, 1:2], thrc[:, 2:3], selc[:, 0:1], op=ALU.mult)
            v.tensor_scalar(selc[:, 2:3], selc[:, 0:1], -1.0, None, op0=ALU.add)
            v.tensor_scalar(selc[:, 3:4], selc[:, 2:3], 1e30, None, op0=ALU.mult)
            v.tensor_tensor(selc[:, 1:2], selc[:, 1:2], selc[:, 3:4], op=ALU.add)
            v.tensor_scalar(selc[:, 4:5], selc[:, 2:3], -1.0, None, op0=ALU.mult)
            v.tensor_tensor(selc[:, 5:6], thrc[:, 2:3], selc[:, 4:5], op=ALU.mult)
            v.tensor_scalar(selc[:, 6:7], selc[:, 0:1], 1e30, None, op0=ALU.mult)
            v.tensor_tensor(selc[:, 5:6], selc[:, 5:6], selc[:, 6:7], op=ALU.add)
            v.tensor_scalar(selc[:, 5:6], selc[:, 5:6], -1.0, None, op0=ALU.mult)
        pb.on("vector", thrs)
        pb.barrier()

        def trsel(t):
            t.matmul(psB[0:1, 0:128], selc[:, 1:2], idf32[:], start=True, stop=True)
            t.matmul(psB[0:1, 128:256], selc[:, 5:6], idf32[:], start=True, stop=True)
        pb.on("tensor", trsel)
        pb.barrier()

        def newb(v):
            v.reduce_max(rowt[0:1, 6:7], psB[0:1, 0:128], axis=AX.X)
            v.reduce_max(rowt[0:1, 7:8], psB[0:1, 128:256], axis=AX.X)
            v.tensor_tensor(rowt[0:1, 1:2], rowt[0:1, 1:2], rowt[0:1, 6:7], op=ALU.max)
            v.tensor_scalar(rowt[0:1, 8:9], rowt[0:1, 0:1], -1.0, None, op0=ALU.mult)
            v.tensor_tensor(rowt[0:1, 8:9], rowt[0:1, 8:9], rowt[0:1, 7:8], op=ALU.max)
            v.tensor_scalar(rowt[0:1, 0:1], rowt[0:1, 8:9], -1.0, None, op0=ALU.mult)
        pb.on("vector", newb)
        pb.barrier()

    pb.on("tensor", lambda t: t.matmul(psA[:, 0:1], onesf[0:1, :], rowt[0:1, 1:2], start=True, stop=True))
    pb.barrier()

    def mkmask(v):
        v.tensor_copy(cnt1[:, 0:1], psA[:, 0:1])
        v.tensor_scalar(cnt1[:, 1:2], cnt1[:, 0:1], -1.0, None, op0=ALU.mult)
        v.tensor_scalar(maskc[:], impP[:], cnt1[:, 1:2], None, op0=ALU.add)
        v.tensor_scalar(maskc[:], maskc[:], 0.0, None, op0=ALU.is_gt)
        v.tensor_copy(mask16[:], maskc[:])
    pb.on("vector", mkmask)
    pb.barrier()
    pb.on("tensor", lambda t: t.matmul(psA[:, 0:32], ltb[:], mask16[:], start=True, stop=True))
    pb.barrier()
    pb.on("vector", lambda v: v.tensor_copy(posw[:], psA[:, 0:32]))
    pb.barrier()
    pb.on("tensor", lambda t: t.matmul(psB[0:1, 0:32], idf32[:, 127:128], posw[:], start=True, stop=True))
    pb.barrier()

    def offsv(v):
        v.tensor_copy(rowt[0:1, 32:64], psB[0:1, 0:32])
        v.tensor_tensor_scan(rowt[0:1, 64:96], rowt[0:1, 32:64], rowt[0:1, 32:64], 0.0,
                             op0=ALU.add, op1=ALU.bypass)
        v.tensor_scalar(rowt[0:1, 96:128], rowt[0:1, 32:64], -1.0, None, op0=ALU.mult)
        v.tensor_tensor(rowt[0:1, 64:96], rowt[0:1, 64:96], rowt[0:1, 96:128], op=ALU.add)
    pb.on("vector", offsv)
    pb.barrier()
    pb.on("tensor", lambda t: t.matmul(psB[:, 32:64], onesf[0:1, :], rowt[0:1, 64:96], start=True, stop=True))
    pb.barrier()

    def posfin(v):
        v.tensor_copy(cscr[:, 0:32], psB[:, 32:64])
        v.tensor_tensor(posw[:], posw[:], cscr[:, 0:32], op=ALU.add)
        v.tensor_tensor(posm[:], posw[:], maskc[:], op=ALU.mult)
        v.tensor_scalar(cscr[:, 0:32], maskc[:], -1.0, None, op0=ALU.add)
        v.tensor_tensor(posm[:], posm[:], cscr[:, 0:32], op=ALU.add)
        v.tensor_scalar(cscr[:, 0:32], posw[:], -(HH + 0.5), None, op0=ALU.add)
        v.tensor_scalar(cscr[:, 0:32], cscr[:, 0:32], 0.0, None, op0=ALU.is_gt)
        v.tensor_scalar(maskc[:], cscr[:, 0:32], -1.0, None, op0=ALU.mult)
        v.tensor_scalar(maskc[:], maskc[:], 1.0, None, op0=ALU.add)
        v.tensor_tensor(posm[:], posm[:], maskc[:], op=ALU.mult)
        v.tensor_tensor(posm[:], posm[:], cscr[:, 0:32], op=ALU.subtract)
        v.tensor_scalar(negposm[:], posm[:], -1.0, None, op0=ALU.mult)
        v.tensor_scalar(cnt1[:, 5:6], iotap[:, 0:1], -1.0, None, op0=ALU.mult)
        v.tensor_scalar(cnt1[:, 5:6], cnt1[:, 5:6], float(HH - 4 * 96) - 0.5, None, op0=ALU.add)
        v.tensor_scalar(cnt1[:, 5:6], cnt1[:, 5:6], 0.0, None, op0=ALU.is_gt)
        v.tensor_copy(valid16[:], cnt1[:, 5:6])
    pb.on("vector", posfin)
    pb.barrier()

    tI1 = tap("posm", (128, 32))
    tI2 = tap("impP", (128, 32))
    tI3 = tap("improw", (1, S))
    if tI1 is not None:
        pb.dma("gpsimd", tI1, posm[:], "st")
    if tI2 is not None:
        pb.dma("gpsimd", tI2, impP[:], "st")
    if tI3 is not None:
        pb.dma("gpsimd", tI3, improw[64:65, :], "st")
    if upto == "I":
        pb.sem("st")
        pb.wait("gpsimd", "st", at=pb.cnt.get("st", 0))
        pb.barrier()
        esI.close()
        esIk.close()
        esP.close()
        pb.finalize()
        return nc, dr
    esI.close()
    pb.end_phase("i_")

    esC = ExitStack()
    vhh = pb.sb("vhh", [128, 4 * 1040], BF16, esC)
    vcn = pb.sb("vcn", [128, 4 * 1040], BF16, esC)
    khh = pb.sb("khh", [128, 4 * 1024], BF16, esC)
    kcn = pb.sb("kcn", [128, 4 * 1024], BF16, esC)
    ustore = pb.sb("ustore", [128, 128], F32, esC)
    u16all = pb.sb("u16all", [128, 1024], BF16, esC)
    esC2 = ExitStack()
    pbtC = pb.ps("pbtC", [128, 1024], BF16, esC2)
    ktcol = [pb.sb(f"ktcol{i}", [128, 8 * 128], BF16, esC2) for i in range(2)]
    knch = [pb.sb(f"knch{i}", [128, 1024], BF16, esC2) for i in range(2)]
    vchb = [pb.sb(f"vchb{i}", [128, 1024], BF16, esC2) for i in range(2)]
    sqf = pb.sb("sqf", [128, 1024], F32, esC2)
    n2all = pb.sb("n2all", [128, 1024], F32, esC2)
    ncb = pb.sb("ncb", [128, 32], F32, esC2)
    uall = pb.sb("uall", [128, 1024], F32, esC2)
    cbt = pb.sb("cbt", [128, 32], F32, esC2)
    cscrC = pb.sb("cscrC", [128, 128], F32, esC2)
    crow = pb.sb("crow", [1, 32], F32, esC2)
    wtot = pb.sb("wtot", [1, 32], F32, esC2)
    wsc = pb.sb("wsc", [128, 32], F32, esC2)
    denb = pb.sb("denb", [128, 128], F32, esC2)
    rd32 = pb.sb("rd32", [128, 128], F32, esC2)
    rd16 = pb.sb("rd16", [128, 128], BF16, esC2)
    kub = pb.sb("kub", [128, 1024], BF16, esC2)
    uvb = pb.sb("uvb", [128, 1024], BF16, esC2)

    for jc in range(32):
        buf = jc % 2
        if jc >= 2:
            pb.wait("sync", "c_tp", at=jc - 1)
            pb.wait("sync", "c_n2", at=2 * (jc - 1))
        pb.dma("sync", vchb[buf][:], v_spill[jc * 128:(jc + 1) * 128, :], "c_kl")
        lat = pb.dma("sync", ktcol[buf][:].rearrange("p (pc j) -> p pc j", pc=8),
                     kt_spill[:, jc * 128:(jc + 1) * 128].rearrange("(pc p) j -> p pc j", p=128),
                     "c_kl")
        pb.wait("tensor", "c_kl", at=lat)
        if jc >= 1:
            pb.wait("tensor", "c_ev", at=jc)

        def ktr(t, buf=buf):
            for pc in range(8):
                t.transpose(pbtC[:, pc * 128:(pc + 1) * 128],
                            ktcol[buf][:, pc * 128:(pc + 1) * 128], ident[:])
        pb.on("tensor", ktr)
        pb.inc("tensor", "c_tp")
        pb.wait("vector", "c_tp", at=jc + 1)
        if jc >= 2:
            pb.wait("vector", "c_sq", at=2 * (jc - 1))
            pb.wait("vector", "c_ks", at=16 * (jc - 1))
        pb.on("vector", lambda v, buf=buf: v.tensor_copy(knch[buf][:], pbtC[:]))
        pb.inc("vector", "c_ev")
        pb.wait("gpsimd", "c_ev", at=jc + 1)
        pb.dma("gpsimd", knat_spill[jc * 128:(jc + 1) * 128, :], knch[buf][:], "c_ks")
        pb.wait("scalar", "c_ev", at=jc + 1)
        pb.wait("scalar", "c_kl", at=lat)
        if jc >= 1:
            pb.wait("scalar", "c_n2", at=2 * jc)
        pb.on("scalar", lambda s, buf=buf: s.activation(sqf[:], knch[buf][:], AF.Square))
        pb.inc("scalar", "c_sq")
        pb.wait("vector", "c_sq", at=2 * jc + 1)
        pb.on("vector", lambda v, jc=jc: v.tensor_reduce(
            n2all[:, jc * 16:(jc + 1) * 16],
            sqf[:].rearrange("p (h d) -> p h d", h=16), axis=AX.X, op=ALU.add))
        pb.inc("vector", "c_n2")
        pb.wait("scalar", "c_n2", at=2 * jc + 1)
        pb.on("scalar", lambda s, buf=buf: s.activation(sqf[:], vchb[buf][:], AF.Square))
        pb.inc("scalar", "c_sq")
        pb.wait("vector", "c_sq", at=2 * jc + 2)
        pb.on("vector", lambda v, jc=jc: v.tensor_reduce(
            n2all[:, 512 + jc * 16:512 + (jc + 1) * 16],
            sqf[:].rearrange("p (h d) -> p h d", h=16), axis=AX.X, op=ALU.add))
        pb.inc("vector", "c_n2")
    pb.barrier()

    pb.on("scalar", lambda s: s.activation(n2all[:], n2all[:], AF.Sqrt))
    pb.barrier()

    def ncbv(v):
        v.tensor_copy(ncb[:, 0:16], n2all[:, 0:16])
        v.tensor_copy(ncb[:, 16:32], n2all[:, 512:528])
    pb.on("vector", ncbv)
    pb.barrier()
    pb.on("tensor", lambda t: t.matmul(psB[0:32, 0:128], ncb[:], idf32[:], start=True, stop=True))
    pb.barrier()

    def cmx(v):
        v.tensor_copy(cscrC[0:32, :], psB[0:32, 0:128])
        v.reduce_max(ncb[0:32, 32:33] if False else cscrC[0:32, 0:1], cscrC[0:32, :], axis=AX.X)
    pb.on("vector", cmx)
    pb.ops["vector"].pop()

    def cmx2(v):
        v.tensor_copy(cscrC[0:32, :], psB[0:32, 0:128])
    pb.on("vector", cmx2)
    pb.on("vector", lambda v: v.reduce_max(wsc[0:32, 0:1], cscrC[0:32, :], axis=AX.X))
    pb.barrier()
    pb.on("tensor", lambda t: t.matmul(psB[0:1, 128:160], wsc[0:32, 0:1], idf32[0:32, 0:32], start=True, stop=True))
    pb.barrier()
    pb.on("vector", lambda v: v.tensor_copy(crow[:], psB[0:1, 128:160]))
    pb.barrier()
    pb.on("tensor", lambda t: t.matmul(psA[:, 0:32], onesf[0:1, :], crow[:], start=True, stop=True))
    pb.barrier()

    def ush(v):
        v.tensor_copy(cbt[:], psA[:, 0:32])
        v.tensor_tensor(n2all[:, 0:512].rearrange("p (c h) -> p c h", c=32),
                        n2all[:, 0:512].rearrange("p (c h) -> p c h", c=32),
                        bass.AP(cbt[:].tensor, 0, [[32, 128], [0, 32], [1, 16]]), op=ALU.subtract)
        v.tensor_tensor(n2all[:, 512:1024].rearrange("p (c h) -> p c h", c=32),
                        n2all[:, 512:1024].rearrange("p (c h) -> p c h", c=32),
                        bass.AP(cbt[:].tensor, 16, [[32, 128], [0, 32], [1, 16]]), op=ALU.subtract)
    pb.on("vector", ush)
    pb.barrier()
    pb.on("scalar", lambda s: s.activation(uall[:], n2all[:], AF.Exp))
    pb.barrier()
    pb.on("vector", lambda v: v.tensor_copy(u16all[:], uall[:]))
    pb.barrier()
    pb.on("tensor", lambda t: t.matmul(psA[0:1, 0:512], onesf[:, 0:1], uall[:, 0:512], start=True, stop=True))
    pb.on("tensor", lambda t: t.matmul(psB[0:1, 0:512], onesf[:, 0:1], uall[:, 512:1024], start=True, stop=True))
    pb.barrier()

    def wfold(v):
        v.reduce_sum(wtot[0:1, 0:16], psA[0:1, 0:512].rearrange("p (c h) -> p h c", c=32), axis=AX.X)
        v.reduce_sum(wtot[0:1, 16:32], psB[0:1, 0:512].rearrange("p (c h) -> p h c", c=32), axis=AX.X)
    pb.on("vector", wfold)
    pb.barrier()
    pb.on("tensor", lambda t: t.matmul(psA[:, 0:32], onesf[0:1, :], wtot[:], start=True, stop=True))
    pb.barrier()
    pb.on("vector", lambda v: v.tensor_scalar(wsc[:], psA[:, 0:32], 1e-8, None, op0=ALU.mult))
    pb.barrier()

    for jc in range(32):
        buf = jc % 2
        gb, c8 = jc // 8, jc % 8
        if jc >= 2:
            pb.wait("sync", "c2_u", at=jc - 1)
        pb.dma("sync", knch[buf][:], knat_spill[jc * 128:(jc + 1) * 128, :], "c2_l")
        lat = pb.dma("sync", vchb[buf][:], v_spill[jc * 128:(jc + 1) * 128, :], "c2_l")
        pb.wait("vector", "c2_l", at=lat)
        if jc >= 1:
            pb.wait("vector", "c2_mm", at=jc)

        def mku(v, jc=jc, buf=buf):
            u3k = u16all[:, jc * 16:(jc + 1) * 16].to_broadcast([128, 16, 64])
            u3v = u16all[:, 512 + jc * 16:512 + (jc + 1) * 16].to_broadcast([128, 16, 64])
            v.tensor_tensor(kub[:].rearrange("p (h d) -> p h d", h=16),
                            knch[buf][:].rearrange("p (h d) -> p h d", h=16), u3k, op=ALU.mult)
            v.tensor_tensor(uvb[:].rearrange("p (h d) -> p h d", h=16),
                            vchb[buf][:].rearrange("p (h d) -> p h d", h=16), u3v, op=ALU.mult)
        pb.on("vector", mku)
        pb.inc("vector", "c2_u")
        pb.wait("tensor", "c2_u", at=jc + 1)
        if c8 == 0 and gb > 0:
            pb.wait("tensor", "c2_ev", at=gb)

        def pools(t, jc=jc, c8=c8):
            st = p8b[:, c8 * 128:(c8 + 1) * 128]
            s_, e_ = (c8 == 0), (c8 == 7)
            t.matmul(psBIG[:, 0:512], st, kub[:, 0:512], start=s_, stop=e_)
            t.matmul(psBIG[:, 512:1024], st, kub[:, 512:1024], start=s_, stop=e_)
            t.matmul(psBIG[:, 1024:1536], st, uvb[:, 0:512], start=s_, stop=e_)
            t.matmul(psBIG[:, 1536:2048], st, uvb[:, 512:1024], start=s_, stop=e_)
            t.matmul(psA[:, 0:16], st, u16all[:, jc * 16:(jc + 1) * 16], start=s_, stop=e_)
            t.matmul(psB[:, 0:16], st, u16all[:, 512 + jc * 16:512 + (jc + 1) * 16], start=s_, stop=e_)
        pb.on("tensor", pools)
        pb.inc("tensor", "c2_mm")
        if c8 == 7:
            mg = pb.inc("tensor", "c2_gb")
            pb.wait("vector", "c2_gb", at=mg)

            def evgb(v, gb=gb):
                v.tensor_copy(kcn[:, gb * 1024:(gb + 1) * 1024], psBIG[:, 0:1024])
                v.tensor_copy(bass.AP(vcn[:].tensor, gb * 1040, [[4160, 128], [65, 16], [1, 64]]),
                              psBIG[:, 1024:2048].rearrange("p (h d) -> p h d", h=16))
                v.tensor_copy(ustore[:, gb * 16:(gb + 1) * 16], psA[:, 0:16])
                v.tensor_copy(ustore[:, 64 + gb * 16:64 + (gb + 1) * 16], psB[:, 0:16])
            pb.on("vector", evgb)
            pb.inc("vector", "c2_ev")
    pb.barrier()

    def dnorm(v):
        v.tensor_tensor(denb[:, 0:64].rearrange("p (g h) -> p g h", g=4),
                        ustore[:, 0:64].rearrange("p (g h) -> p g h", g=4),
                        bass.AP(wsc[:].tensor, 0, [[32, 128], [0, 4], [1, 16]]), op=ALU.add)
        v.tensor_tensor(denb[:, 64:128].rearrange("p (g h) -> p g h", g=4),
                        ustore[:, 64:128].rearrange("p (g h) -> p g h", g=4),
                        bass.AP(wsc[:].tensor, 16, [[32, 128], [0, 4], [1, 16]]), op=ALU.add)
        v.reciprocal(rd32[:], denb[:])
        v.tensor_copy(rd16[:], rd32[:])
    pb.on("vector", dnorm)
    pb.barrier()

    def cnorm(v):
        for gb in range(4):
            r3k = rd16[:, gb * 16:(gb + 1) * 16].to_broadcast([128, 16, 64])
            r3v = rd16[:, 64 + gb * 16:64 + (gb + 1) * 16].to_broadcast([128, 16, 64])
            kv = kcn[:, gb * 1024:(gb + 1) * 1024].rearrange("p (h d) -> p h d", h=16)
            v.tensor_tensor(kv, kv, r3k, op=ALU.mult)
            vv = bass.AP(vcn[:].tensor, gb * 1040, [[4160, 128], [65, 16], [1, 64]])
            v.tensor_tensor(vv, vv, r3v, op=ALU.mult)
        v.memset(bass.AP(vcn[:].tensor, 64, [[4160, 128], [65, 64], [1, 1]]), 1.0)
    pb.on("vector", cnorm)
    pb.end_phase("c_", "c2_")
    esC2.close()

    tC1 = tap("kcn", (128, 4 * 1024), BF16)
    tC2 = tap("vcn", (128, 4 * 1040), BF16)
    if tC1 is not None:
        pb.dma("gpsimd", tC1, kcn[:], "st")
        pb.dma("gpsimd", tC2, vcn[:], "st")
    if upto == "C":
        pb.sem("st")
        pb.wait("gpsimd", "st", at=pb.cnt.get("st", 0))
        pb.barrier()
        esC.close()
        esIk.close()
        esP.close()
        pb.finalize()
        return nc, dr

    esG = ExitStack()
    psE = pb.ps("psE", [128, 512], F32, esG)
    psF = pb.ps("psF", [128, 512], F32, esG)
    gkn = [pb.sb(f"gkn{i}", [128, 1024], BF16, esG) for i in range(2)]
    gvc = [pb.sb(f"gvc{i}", [128, 1024], BF16, esG) for i in range(2)]
    g1b = pb.sb("g1b", [128, 128], F32, esG)
    G16 = pb.sb("G16", [128, 256], BF16, esG)

    for sw in range(2):
        for jc in range(32):
            buf = jc % 2
            it = sw * 32 + jc
            if it >= 2:
                pb.wait("sync", "g_mm", at=it - 1)
            pb.dma("sync", gkn[buf][:], knat_spill[jc * 128:(jc + 1) * 128, :], "g_l")
            lat = pb.dma("sync", gvc[buf][:], v_spill[jc * 128:(jc + 1) * 128, :], "g_l")
            pb.wait("vector", "g_l", at=lat)
            if it >= 1:
                pb.wait("vector", "g_mm", at=it)

            def gbv(v, jc=jc, sw=sw):
                for il in range(2):
                    ib = 2 * sw + il
                    v.tensor_scalar(g1b[:], iotab[:], negposm[:, jc:jc + 1], None, op0=ALU.add)
                    v.tensor_scalar(g1b[:], g1b[:], float(-(1 + 128 * ib)), None, op0=ALU.is_equal)
                    v.tensor_copy(G16[:, il * 128:(il + 1) * 128], g1b[:])
            pb.on("vector", gbv)
            gv = pb.inc("vector", "g_v")
            pb.wait("tensor", "g_v", at=gv)
            if jc == 0 and sw == 1:
                pb.wait("tensor", "g_ev", at=1)

            def gmm(t, jc=jc, buf=buf):
                st0, st1 = G16[:, 0:128], G16[:, 128:256]
                s_, e_ = (jc == 0), (jc == 31)
                t.matmul(psBIG[:, 0:512], st0, gkn[buf][:, 0:512], start=s_, stop=e_)
                t.matmul(psBIG[:, 512:1024], st0, gkn[buf][:, 512:1024], start=s_, stop=e_)
                t.matmul(psBIG[:, 1024:1536], st0, gvc[buf][:, 0:512], start=s_, stop=e_)
                t.matmul(psBIG[:, 1536:2048], st0, gvc[buf][:, 512:1024], start=s_, stop=e_)
                t.matmul(psA[:], st1, gkn[buf][:, 0:512], start=s_, stop=e_)
                t.matmul(psB[:], st1, gkn[buf][:, 512:1024], start=s_, stop=e_)
                t.matmul(psE[:], st1, gvc[buf][:, 0:512], start=s_, stop=e_)
                t.matmul(psF[:], st1, gvc[buf][:, 512:1024], start=s_, stop=e_)
            pb.on("tensor", gmm)
            pb.inc("tensor", "g_mm")
        gm = pb.cnt["g_mm"]
        pb.wait("vector", "g_mm", at=gm)

        def gev(v, sw=sw):
            ib0, ib1 = 2 * sw, 2 * sw + 1
            v.tensor_copy(khh[:, ib0 * 1024:ib0 * 1024 + 512], psBIG[:, 0:512])
            v.tensor_copy(khh[:, ib0 * 1024 + 512:(ib0 + 1) * 1024], psBIG[:, 512:1024])
            v.tensor_copy(bass.AP(vhh[:].tensor, ib0 * 1040, [[4160, 128], [65, 8], [1, 64]]),
                          psBIG[:, 1024:1536].rearrange("p (h d) -> p h d", h=8))
            v.tensor_copy(bass.AP(vhh[:].tensor, ib0 * 1040 + 8 * 65, [[4160, 128], [65, 8], [1, 64]]),
                          psBIG[:, 1536:2048].rearrange("p (h d) -> p h d", h=8))
            v.tensor_copy(khh[:, ib1 * 1024:ib1 * 1024 + 512], psA[:])
            v.tensor_copy(khh[:, ib1 * 1024 + 512:(ib1 + 1) * 1024], psB[:])
            v.tensor_copy(bass.AP(vhh[:].tensor, ib1 * 1040, [[4160, 128], [65, 8], [1, 64]]),
                          psE[:].rearrange("p (h d) -> p h d", h=8))
            v.tensor_copy(bass.AP(vhh[:].tensor, ib1 * 1040 + 8 * 65, [[4160, 128], [65, 8], [1, 64]]),
                          psF[:].rearrange("p (h d) -> p h d", h=8))
        pb.on("vector", gev)
        pb.inc("vector", "g_ev")
    pb.on("vector", lambda v: v.memset(bass.AP(vhh[:].tensor, 64, [[4160, 128], [65, 64], [1, 1]]), 1.0))
    pb.end_phase("g_")
    esG.close()

    tG1 = tap("khh", (128, 4 * 1024), BF16)
    tG2 = tap("vhh", (128, 4 * 1040), BF16)
    if tG1 is not None:
        pb.dma("gpsimd", tG1, khh[:], "st")
        pb.dma("gpsimd", tG2, vhh[:], "st")
    if upto == "G":
        pb.sem("st")
        pb.wait("gpsimd", "st", at=pb.cnt.get("st", 0))
        pb.barrier()
        esC.close()
        esIk.close()
        esP.close()
        pb.finalize()
        return nc, dr

    esT = ExitStack()
    kcatT = pb.sb("kcatT", [128, 8 * 1024], BF16, esT)
    wo_bf = pb.sb("wo_bf", [128, 8 * 1024], BF16, esT)
    p2b = pb.sb("p2b", [128, 16 * 512], BF16, esT)
    attnT8 = pb.sb("attnT8", [128, 8 * 512], BF16, esT)
    z2r = pb.sb("z2r", [128, 1024], F32, esT)
    rz2f = pb.sb("rz2f", [128, 1024], F32, esT)
    rz216 = pb.sb("rz216", [128, 1024], BF16, esT)
    rzst2 = pb.sb("rzst2", [128, 512], BF16, esT)
    obst = pb.sb("obst", [128, 512], BF16, esT)
    oxs = [pb.sb(f"oxs{i}", [128, 1024], F32, esT) for i in range(2)]
    offn = [pb.sb(f"offn{i}", [128, 1024], F32, esT) for i in range(2)]
    ostg = [pb.sb(f"ostg{i}", [128, 1024], F32, esT) for i in range(2)]

    woat = pb.dma("gpsimd", wo_bf[:].rearrange("p (pc m) -> p pc m", pc=8),
                  wo_in.rearrange("(pc p) m -> p pc m", p=128), "t_w")

    esT2 = ExitStack()
    pbtT = pb.ps("pbtT", [128, 1024], BF16, esT2)
    for p in range(8):
        if p >= 1:
            pb.wait("tensor", "t_pe", at=p)

        def ktt(t, p=p):
            for kb in range(8):
                for h in range(2):
                    c0 = (2 * p + h) * 64
                    if kb < 4:
                        src = khh[:, kb * 1024 + c0: kb * 1024 + c0 + 64]
                    else:
                        src = kcn[:, (kb - 4) * 1024 + c0: (kb - 4) * 1024 + c0 + 64]
                    if h == 0:
                        t.transpose(pbtT[0:64, kb * 128:(kb + 1) * 128], src, ident[:])
                    else:
                        t.transpose(pbtT[64:128, kb * 128:(kb + 1) * 128], src, ident[:],
                                    tile_position=(0, 64))
        pb.on("tensor", ktt)
        tp = pb.inc("tensor", "t_pt")
        pb.wait("vector", "t_pt", at=tp)
        pb.on("vector", lambda v, p=p: v.tensor_copy(kcatT[:, p * 1024:(p + 1) * 1024], pbtT[:]))
        pb.inc("vector", "t_pe")
    pb.barrier()
    esT2.close()
    esT3 = ExitStack()
    psC2 = pb.ps("psC2", [128, 512], F32, esT3)
    psD2 = pb.ps("psD2", [128, 512], F32, esT3)
    pb.wait("tensor", "t_w", at=woat)
    pb.wait("vector", "t_w", at=woat)

    rounds = [0]
    for qw in range(4):
        for p in range(8):
            gidx = qw * 8 + p
            for kb in range(8):
                rid = rounds[0]
                rounds[0] += 1
                half = kb % 2
                if rid >= 2:
                    pb.wait("tensor", "t_ex", at=rid - 1)
                if kb == 0 and qw >= 1 and p == 0:
                    pb.wait("tensor", "o_ev", at=4 * qw)

                def lmm(t, p=p, qw=qw, kb=kb, half=half):
                    for h in range(2):
                        t.matmul(psBIG[:, (2 * half + h) * 512:(2 * half + h + 1) * 512],
                                 kcatT[h * 64:(h + 1) * 64, p * 1024 + kb * 128: p * 1024 + (kb + 1) * 128],
                                 qT_pair(p)[h * 64:(h + 1) * 64, qw * 512:(qw + 1) * 512],
                                 start=True, stop=True, tile_position=(64 * h, 0))
                pb.on("tensor", lmm)
                tm = pb.inc("tensor", "t_mm")
                pb.wait("scalar", "t_mm", at=tm)

                def ex(s, kb=kb, half=half):
                    outap = bass.AP(p2b[:].tensor, kb * 512, [[16 * 512, 128], [8 * 512, 2], [1, 512]])
                    s.activation(outap, psBIG[:, 2 * half * 512:(2 * half + 2) * 512], AF.Exp, scale=SCALE)
                pb.on("scalar", ex)
                pb.inc("scalar", "t_ex")
            ex8 = pb.cnt["t_ex"]
            pb.wait("vector", "t_ex", at=ex8)

            def msk(v):
                for h in range(2):
                    v.tensor_scalar(p2b[:, (h * 8 + 3) * 512:(h * 8 + 4) * 512],
                                    p2b[:, (h * 8 + 3) * 512:(h * 8 + 4) * 512],
                                    valid16[:, 0:1], None, op0=ALU.mult)
            pb.on("vector", msk)
            vm = pb.inc("vector", "t_mk")
            pb.wait("tensor", "t_mk", at=vm)
            if gidx >= 1:
                pb.wait("tensor", "t_ev", at=gidx)

            def av(t, p=p):
                for h in range(2):
                    bank = psA if h == 0 else psB
                    for kb in range(8):
                        c0 = (2 * p + h) * 65
                        if kb < 4:
                            vs = vhh[:, kb * 1040 + c0: kb * 1040 + c0 + 65]
                        else:
                            vs = vcn[:, (kb - 4) * 1040 + c0: (kb - 4) * 1040 + c0 + 65]
                        t.matmul(bank[0:65, :], vs, p2b[:, (h * 8 + kb) * 512:(h * 8 + kb + 1) * 512],
                                 start=(kb == 0), stop=(kb == 7))
            pb.on("tensor", av)
            ta = pb.inc("tensor", "t_av")
            pb.wait("vector", "t_av", at=ta)

            def nv(v):
                v.tensor_copy(z2r[64:65, 0:512], psA[64:65, :])
                v.tensor_copy(z2r[64:65, 512:1024], psB[64:65, :])
                v.reciprocal(rz2f[64:65, :], z2r[64:65, :])
                v.tensor_copy(rz216[64:65, :], rz2f[64:65, :])
                v.tensor_copy(obst[0:64, :], psB[0:64, :])
            pb.on("vector", nv)
            nv_ = pb.inc("vector", "t_nv")
            pb.wait("tensor", "t_nv", at=nv_)

            def rzm(t):
                t.matmul(psC2[0:64, :], onesb[64:65, 0:64], rz216[64:65, 0:512],
                         start=True, stop=True, tile_position=(64, 0))
                t.matmul(psC2[64:128, :], onesb[64:65, 64:128], rz216[64:65, 512:1024],
                         start=True, stop=True, tile_position=(64, 64))
                t.matmul(psD2[64:128, :], ident[0:64, 0:64], obst[0:64, :],
                         start=True, stop=True, tile_position=(0, 64))
            pb.on("tensor", rzm)
            tr = pb.inc("tensor", "t_rz")
            pb.wait("vector", "t_rz", at=tr)
            if qw >= 1:
                pb.wait("vector", "t_wm", at=qw)

            def stk(v, p=p):
                v.tensor_copy(rzst2[:], psC2[:])
                v.tensor_copy(attnT8[0:64, p * 512:(p + 1) * 512], psA[0:64, :])
                v.tensor_copy(attnT8[64:128, p * 512:(p + 1) * 512], psD2[64:128, :])
                v.tensor_tensor(attnT8[:, p * 512:(p + 1) * 512],
                                attnT8[:, p * 512:(p + 1) * 512], rzst2[:], op=ALU.mult)
            pb.on("vector", stk)
            pb.inc("vector", "t_ev")
        ev8 = pb.cnt["t_ev"]
        for jb in range(4):
            cix = qw * 4 + jb
            bufo = cix % 2
            if cix >= 2:
                pb.wait("sync", "o_dn", at=cix - 1)
                pb.wait("gpsimd", "o_dn", at=cix - 1)
            lx = pb.dma("sync", oxs[bufo][:], x_in[qw * 512 + jb * 128: qw * 512 + (jb + 1) * 128, :], "o_x")
            lf = pb.dma("gpsimd", offn[bufo][:], ffn_spill[qw * 512 + jb * 128: qw * 512 + (jb + 1) * 128, :], "o_f")
            pb.wait("tensor", "t_ev", at=ev8)
            if cix >= 1:
                pb.wait("tensor", "o_ev", at=cix)

            def wom(t, jb=jb):
                for mw in range(2):
                    for p in range(8):
                        t.matmul(psBIG[:, mw * 512:(mw + 1) * 512],
                                 attnT8[:, p * 512 + jb * 128: p * 512 + (jb + 1) * 128],
                                 wo_bf[:, p * 1024 + mw * 512: p * 1024 + (mw + 1) * 512],
                                 start=(p == 0), stop=(p == 7))
            pb.on("tensor", wom)
            wm_ = pb.inc("tensor", "o_mm")
            if jb == 3:
                pb.inc("tensor", "t_wm")
            pb.wait("vector", "o_mm", at=wm_)
            pb.wait("vector", "o_x", at=lx)
            pb.wait("vector", "o_f", at=lf)
            if cix >= 2:
                pb.wait("vector", "o_st", at=16 * (cix - 1))

            def oadd(v, bufo=bufo):
                v.tensor_tensor(ostg[bufo][:, 0:512], psBIG[:, 0:512], offn[bufo][:, 0:512], op=ALU.add)
                v.tensor_tensor(ostg[bufo][:, 512:1024], psBIG[:, 512:1024], offn[bufo][:, 512:1024], op=ALU.add)
                v.tensor_tensor(ostg[bufo][:], ostg[bufo][:], oxs[bufo][:], op=ALU.add)
            pb.on("vector", oadd)
            ov = pb.inc("vector", "o_ev")
            pb.wait("gpsimd", "o_ev", at=ov)
            pb.dma("gpsimd", out[qw * 512 + jb * 128: qw * 512 + (jb + 1) * 128, :], ostg[bufo][:], "o_st")
            pb.inc("gpsimd", "o_dn")

    ofin = pb.cnt["o_st"]
    for e in ENGINES:
        pb.wait(e, "o_st", at=ofin)
    pb.barrier()
    esT3.close()
    esT.close()
    esC.close()
    esIk.close()
    esP.close()
    pb.finalize()
    return nc, dr


def build_consts():
    ident = np.eye(128, dtype=np.float32)
    lt = (np.arange(128)[:, None] <= np.arange(128)[None, :]).astype(np.float32)
    iota128 = np.broadcast_to(np.arange(128, dtype=np.float32), (128, 128)).copy()
    p8 = np.zeros((8, 128, 128), np.float32)
    for c in range(8):
        for j in range(128):
            p8[c, j, 16 * c + j // 8] = 1.0
    return ident, lt, iota128, p8.reshape(8 * 128, 128)


def perm_tokens(r):
    if r == 0:
        return np.arange(S)
    return np.concatenate([np.arange(SH, S), np.arange(0, SH)])


def perm_heads(r):
    pairs = list(range(8))
    if r == 1:
        pairs = pairs[4:] + pairs[:4]
    heads = []
    for p in pairs:
        heads += [2 * p, 2 * p + 1]
    return np.array(heads)


def core_inputs(inp, c):
    b, r = c >> 1, c & 1
    tp = perm_tokens(r)
    hp = perm_heads(r)
    hcols = (hp[:, None] * HD + np.arange(HD)[None, :]).reshape(-1)
    ident, lt, iota128, p8 = build_consts()
    return {
        "x": np.ascontiguousarray(np.asarray(inp["L0_attn_x"])[b][tp]),
        "wq": np.ascontiguousarray(np.asarray(inp["L0_attn_Wq"])[:, hcols]),
        "wk": np.ascontiguousarray(np.asarray(inp["L0_attn_Wk"])[:, hcols]),
        "wv": np.ascontiguousarray(np.asarray(inp["L0_attn_Wv"])[:, hcols]),
        "wo": np.ascontiguousarray(np.asarray(inp["L0_attn_Wo"])[hcols, :]),
        "wg": np.ascontiguousarray(np.asarray(inp["L0_ffn_w_gate"])),
        "wu": np.ascontiguousarray(np.asarray(inp["L0_ffn_w_up"])),
        "wd": np.ascontiguousarray(np.asarray(inp["L0_ffn_w_down"])),
        "cident": ident, "clt": lt, "ciota": iota128, "cp8": p8,
        "half_off": np.array([[float(SH * r)]], np.float32),
        "hoffi": np.array([[SH * r, SH - SH * r]], np.int32),
    }


def kernel(**inputs):
    nc, _ = build_program(upto="Z")
    in_maps = [core_inputs(inputs, c) for c in range(8)]
    res = run_bass_kernel_spmd(nc, in_maps, core_ids=list(range(8)))
    outf = np.empty((B, S, D), np.float32)
    for c in range(8):
        b, r = c >> 1, c & 1
        outf[b, r * SH:(r + 1) * SH] = res.results[c]["out"]
    return outf
